# version 01 (final)
# baseline (speedup 1.0000x reference)
import numpy as np
import math
import concourse.bass as bass
from concourse.bass import IndirectOffsetOnAxis
import concourse.mybir as mybir
from concourse.bass_utils import run_bass_kernel_spmd

F32 = mybir.dt.float32
I32 = mybir.dt.int32
AF = mybir.ActivationFunctionType
ALU = mybir.AluOpType
AX = mybir.AxisListType

B, S, D, N, R = 2, 8192, 1024, 16, 64
KAPPA = 0.5
TOK = B * S
TPC = TOK // 8
TH = TPC // 2
LAMBDA_INIT = 0.8 - 0.6 * float(np.exp(-0.3 * 12))
CH = 64


class Script:
    ENGINES = ("sync", "vector", "scalar", "tensor", "gpsimd")

    def __init__(self):
        self.items = {e: [] for e in self.ENGINES}
        self.cnt = {}

    def n(self, sem):
        return self.cnt.get(id(sem), 0)

    def bump(self, sem, amt):
        self.cnt[id(sem)] = self.n(sem) + amt
        return self.n(sem)

    def add(self, eng, fn, waits=(), inc=None):
        waits = tuple(waits)
        val = None
        sem = None
        amt = 0
        if inc is not None:
            sem, amt = inc
            val = self.bump(sem, amt)

        def item(h, fn=fn, waits=waits, sem=sem, amt=amt):
            for s_, v_ in waits:
                h.wait_ge(s_, v_)
            inst = fn(h)
            if sem is not None:
                inst.then_inc(sem, amt)
        self.items[eng].append(item)
        return val

    def emit(self, block):
        items = self.items

        @block.sync
        def _(h):
            for it in items["sync"]:
                it(h)

        @block.vector
        def _(h):
            for it in items["vector"]:
                it(h)

        @block.scalar
        def _(h):
            for it in items["scalar"]:
                it(h)

        @block.tensor
        def _(h):
            for it in items["tensor"]:
                it(h)

        @block.gpsimd
        def _(h):
            for it in items["gpsimd"]:
                it(h)


def _mk(nc, ctx, kind, name, shape, dtype=F32):
    t = {"sb": nc.sbuf_tensor, "ps": nc.psum_tensor}[kind](name, shape, dtype)
    ctx.append(t)
    return t.__enter__()


def _sem(nc, ctx, name):
    t = nc.semaphore(name)
    ctx.append(t)
    return t.__enter__()



def build_pre(nc):
    ctx = []
    sb = lambda n_, s_, d_=F32: _mk(nc, ctx, "sb", n_, s_, d_)
    ps = lambda n_, s_: _mk(nc, ctx, "ps", n_, s_)

    xT = nc.dram_tensor("xT", [D, TPC], F32, kind="ExternalInput").ap()
    W_in = nc.dram_tensor("W_in", [D, 2 * D], F32, kind="ExternalInput").ap()
    W_xp = nc.dram_tensor("W_xp", [D, 128], F32, kind="ExternalInput").ap()
    W_dt = nc.dram_tensor("W_dt", [R, D], F32, kind="ExternalInput").ap()
    b_dtI = nc.dram_tensor("b_dt", [1, D], F32, kind="ExternalInput").ap()
    Wres_r = nc.dram_tensor("Wres_r", [D, 16], F32, kind="ExternalInput").ap()
    Wres_i = nc.dram_tensor("Wres_i", [D, 16], F32, kind="ExternalInput").ap()
    gate = nc.dram_tensor("gate", [D, 8], F32, kind="ExternalInput").ap()
    ident = nc.dram_tensor("ident", [128, 128], F32, kind="ExternalInput").ap()

    xnF = nc.dram_tensor("xnF", [D, TPC], F32, kind="ExternalOutput").ap()
    xnT = nc.dram_tensor("xnT", [TPC, D], F32, kind="ExternalOutput").ap()
    zF = nc.dram_tensor("zF", [D, TPC], F32, kind="ExternalOutput").ap()
    dtT = nc.dram_tensor("dtT", [TPC, D], F32, kind="ExternalOutput").ap()
    dtxT = nc.dram_tensor("dtxT", [TPC, D], F32, kind="ExternalOutput").ap()
    bcF = nc.dram_tensor("bcF", [64, TPC], F32, kind="ExternalOutput").ap()
    uxrF = nc.dram_tensor("uxrF", [16, TPC], F32, kind="ExternalOutput").ap()
    uxiF = nc.dram_tensor("uxiF", [16, TPC], F32, kind="ExternalOutput").ap()
    probsT = nc.dram_tensor("probsT", [TPC, 8], F32, kind="ExternalOutput").ap()

    xfm = sb("xfm", [128, 8 * TH])
    xnf = sb("xnf", [128, 8 * TH])
    xif = xfm
    xit = sb("xit", [128, 8 * TH])
    stage = sb("stage", [128, 512])
    sqch = sb("sqch", [128, TH])
    sums = sb("sums", [1, 2 * TH])
    stats = sb("stats", [1, 2 * TH])
    mub = sb("mub", [128, TH])
    rsb = sb("rsb", [128, TH])
    bdtb = sb("bdtb", [128, D])
    onesc = sb("onesc", [128, 1])
    onesr = sb("onesr", [1, 128])
    idt = sb("idt", [128, 128])
    wbuf = sb("wbuf", [128, 8 * 128])
    wxp = sb("wxp", [128, 128 * 8])
    wdt = sb("wdt", [64, D])
    wrr = sb("wrr", [128, 16 * 8])
    wri = sb("wri", [128, 16 * 8])
    gt = sb("gt", [128, 8 * 8])
    xp64 = sb("xp64", [64, TH])
    xresr = sb("xresr", [16, TH])
    xresi = sb("xresi", [16, TH])
    hr_ = sb("hr_", [16, TH])
    sqr_ = sb("sqr_", [16, TH])
    sqi_ = sb("sqi_", [16, TH])
    irc_ = sb("irc_", [16, TH])
    dtile = sb("dtile", [128, 512])
    atile = sb("atile", [128, 512])
    btile = sb("btile", [128, 512])
    smx = sb("smx", [128, 1])
    sprob = sb("sprob", [128, 8])
    ssum = sb("ssum", [128, 1])
    sinv = sb("sinv", [128, 1])
    eps5 = sb("eps5", [1, 1])
    c1b = sb("c1b", [128, 1])

    ps_a = ps("ps_a", [128, 512])
    ps_b = ps("ps_b", [128, 512])
    ps_s = ps("ps_s", [1, 2 * TH])
    ps_m = ps("ps_m", [128, 512])
    ps_t = ps_m[:, 0:128]
    ps_l = ps("ps_l", [128, 8])[:, :]
    ps_x = ps_m[0:16, 0:512]

    ld = _sem(nc, ctx, "p_ld")
    tv = _sem(nc, ctx, "p_tv")
    tsm = _sem(nc, ctx, "p_tsm")
    ttm = _sem(nc, ctx, "p_ttm")
    dout = _sem(nc, ctx, "p_dout")

    sc = Script()
    A = sc.add

    A("sync", lambda h: h.dma_start(out=idt[:, :], in_=ident), inc=(ld, 16))
    A("sync", lambda h: h.dma_start(out=wdt[:, :], in_=W_dt), inc=(ld, 16))
    A("sync", lambda h: h.dma_start(out=bdtb[0:1, :], in_=b_dtI), inc=(ld, 16))
    for dc in range(8):
        A("sync", lambda h, dc=dc: h.dma_start(
            out=wxp[:, dc * 128:(dc + 1) * 128], in_=W_xp[dc * 128:(dc + 1) * 128, :]), inc=(ld, 16))
        A("sync", lambda h, dc=dc: h.dma_start(
            out=wrr[:, dc * 16:(dc + 1) * 16], in_=Wres_r[dc * 128:(dc + 1) * 128, :]), inc=(ld, 16))
        A("sync", lambda h, dc=dc: h.dma_start(
            out=wri[:, dc * 16:(dc + 1) * 16], in_=Wres_i[dc * 128:(dc + 1) * 128, :]), inc=(ld, 16))
        A("sync", lambda h, dc=dc: h.dma_start(
            out=gt[:, dc * 8:(dc + 1) * 8], in_=gate[dc * 128:(dc + 1) * 128, :]), inc=(ld, 16))
    LDC = sc.n(ld)
    A("vector", lambda h: h.memset(onesc[:, :], 1.0))
    A("vector", lambda h: h.memset(onesr[:, :], 1.0))
    A("vector", lambda h: h.memset(eps5[:, :], 1e-5))
    A("vector", lambda h: h.memset(c1b[:, :], 1.0), inc=(tv, 1))
    for nj in range(2):
        t1 = A("tensor", lambda h, nj=nj: h.matmul(
            ps_a[:, 0:512], onesr[:, :], bdtb[0:1, nj * 512:(nj + 1) * 512],
            start=True, stop=True), waits=[(ld, LDC), (tv, sc.n(tv))], inc=(ttm, 1))
        A("vector", lambda h, nj=nj: h.tensor_copy(
            out=stage[:, :], in_=ps_a[:, 0:512]), waits=[(ttm, t1)], inc=(tv, 1))
        A("vector", lambda h, nj=nj: h.tensor_copy(
            out=bdtb[:, nj * 512:(nj + 1) * 512], in_=stage[:, :]), inc=(tv, 1))

    for h2 in range(2):
        t0 = h2 * TH
        if h2 > 0:
            A("vector", lambda h: h.engine_nop(), waits=[(dout, sc.n(dout))])
        for dc in range(8):
            A("sync", lambda h, dc=dc, t0=t0: h.dma_start(
                out=xfm[:, dc * TH:(dc + 1) * TH],
                in_=xT[dc * 128:(dc + 1) * 128, t0:t0 + TH]),
                waits=([(tv, sc.n(tv))] if (h2 > 0 and dc == 0) else ()), inc=(ld, 16))
        LDX = sc.n(ld)
        for dc in range(8):
            vsq = A("vector", lambda h, dc=dc: h.tensor_tensor(
                out=sqch[:, :], in0=xfm[:, dc * TH:(dc + 1) * TH],
                in1=xfm[:, dc * TH:(dc + 1) * TH], op=ALU.mult),
                waits=[(ld, LDX)] + ([(ttm, sc.n(ttm))] if dc > 0 else []), inc=(tv, 1))

            def mm_stats(h, dc=dc):
                last = None
                for nj in range(2):
                    last = h.matmul(ps_s[:, nj * 512:(nj + 1) * 512], onesc[:, :],
                                    xfm[:, dc * TH + nj * 512:dc * TH + (nj + 1) * 512],
                                    start=(dc == 0), stop=(dc == 7))
                    last = h.matmul(ps_s[:, TH + nj * 512:TH + (nj + 1) * 512],
                                    onesc[:, :], sqch[:, nj * 512:(nj + 1) * 512],
                                    start=(dc == 0), stop=(dc == 7))
                return last
            A("tensor", mm_stats, waits=[(tv, vsq)], inc=(ttm, 1))
        TTS = sc.n(ttm)
        A("vector", lambda h: h.tensor_scalar(
            out=stats[:, 0:TH], in0=ps_s[:, 0:TH], scalar1=1.0 / 1024.0, scalar2=None,
            op0=ALU.mult), waits=[(ttm, TTS)])
        A("vector", lambda h: h.tensor_scalar(
            out=sums[:, 0:TH], in0=ps_s[:, TH:2 * TH], scalar1=1.0 / 1024.0,
            scalar2=None, op0=ALU.mult))
        A("vector", lambda h: h.tensor_tensor(
            out=sums[:, TH:2 * TH], in0=stats[:, 0:TH], in1=stats[:, 0:TH], op=ALU.mult))
        vst = A("vector", lambda h: h.tensor_tensor(
            out=sums[:, 0:TH], in0=sums[:, 0:TH], in1=sums[:, TH:2 * TH],
            op=ALU.subtract), inc=(tv, 1))
        A("scalar", lambda h: h.activation(out=sums[:, TH:2 * TH], in_=sums[:, 0:TH],
                                           func=AF.Ln, bias=eps5[:, :]), waits=[(tv, vst)])
        sst = A("scalar", lambda h: h.activation(
            out=stats[:, TH:2 * TH], in_=sums[:, TH:2 * TH], func=AF.Exp, scale=-0.5),
            inc=(tsm, 1))
        for (srcoff, dst) in ((0, mub), (TH, rsb)):
            for nj in range(2):
                tb = A("tensor", lambda h, srcoff=srcoff, nj=nj: h.matmul(
                    ps_a[:, 0:512], onesr[:, :],
                    stats[:, srcoff + nj * 512:srcoff + (nj + 1) * 512],
                    start=True, stop=True),
                    waits=[(tsm, sst), (tv, sc.n(tv))], inc=(ttm, 1))
                A("vector", lambda h, dst=dst, nj=nj: h.tensor_copy(
                    out=dst[:, nj * 512:(nj + 1) * 512], in_=ps_a[:, 0:512]),
                    waits=[(ttm, tb)], inc=(tv, 1))
        for dc in range(8):
            A("vector", lambda h, dc=dc: h.tensor_tensor(
                out=xnf[:, dc * TH:(dc + 1) * TH], in0=xfm[:, dc * TH:(dc + 1) * TH],
                in1=mub[:, :], op=ALU.subtract))
            vxn = A("vector", lambda h, dc=dc: h.tensor_tensor(
                out=xnf[:, dc * TH:(dc + 1) * TH], in0=xnf[:, dc * TH:(dc + 1) * TH],
                in1=rsb[:, :], op=ALU.mult), inc=(tv, 1))
            A("sync", lambda h, dc=dc, t0=t0: h.dma_start(
                out=xnF[dc * 128:(dc + 1) * 128, t0:t0 + TH],
                in_=xnf[:, dc * TH:(dc + 1) * TH]), waits=[(tv, vxn)], inc=(dout, 16))
        for tc in range(TH // 128):
            for dc in range(8):
                tt = A("tensor", lambda h, tc=tc, dc=dc: h.transpose(
                    ps_t, xnf[:, dc * TH + tc * 128:dc * TH + (tc + 1) * 128],
                    idt[:, :]), waits=[(tv, sc.n(tv))], inc=(ttm, 1))
                A("vector", lambda h, tc=tc, dc=dc: h.tensor_copy(
                    out=xit[:, tc * 1024 + dc * 128:tc * 1024 + (dc + 1) * 128],
                    in_=ps_t), waits=[(ttm, tt)], inc=(tv, 1))
            A("sync", lambda h, tc=tc, t0=t0: h.dma_start(
                out=xnT[t0 + tc * 128:t0 + (tc + 1) * 128, :],
                in_=xit[:, tc * 1024:(tc + 1) * 1024]),
                waits=[(tv, sc.n(tv))], inc=(dout, 16))
        for jc in range(16):
            lw = A("sync", lambda h, jc=jc: h.dma_start(
                out=wbuf[:, :].rearrange("p (a c) -> p a c", a=8),
                in_=W_in[:, jc * 128:(jc + 1) * 128].rearrange(
                    "(a p) c -> p a c", p=128)),
                waits=[(ttm, sc.n(ttm))], inc=(ld, 16))
            for nj in range(2):
                def mmx(h, jc=jc, nj=nj):
                    last = None
                    for dc in range(8):
                        last = h.matmul(
                            ps_a[:, 0:512] if nj == 0 else ps_b[:, 0:512],
                            wbuf[:, dc * 128:(dc + 1) * 128],
                            xnf[:, dc * TH + nj * 512:dc * TH + (nj + 1) * 512],
                            start=(dc == 0), stop=(dc == 7))
                    return last
                tmm = A("tensor", mmx,
                        waits=[(ld, sc.n(ld)), (tv, sc.n(tv)), (tsm, sc.n(tsm))],
                        inc=(ttm, 1))
                psrc = (lambda nj=nj: ps_a if nj == 0 else ps_b)()
                if jc < 8:
                    A("scalar", lambda h, jc=jc, nj=nj, psrc=psrc: h.activation(
                        out=xif[:, jc * TH + nj * 512:jc * TH + (nj + 1) * 512],
                        in_=psrc[:, 0:512], func=AF.Silu), waits=[(ttm, tmm)],
                        inc=(tsm, 1))
                else:
                    vz = A("vector", lambda h, psrc=psrc: h.tensor_copy(
                        out=stage[:, :], in_=psrc[:, 0:512]),
                        waits=[(ttm, tmm), (dout, sc.n(dout))], inc=(tv, 1))
                    A("sync", lambda h, jc=jc, nj=nj, t0=t0: h.dma_start(
                        out=zF[(jc - 8) * 128:(jc - 7) * 128,
                               t0 + nj * 512:t0 + (nj + 1) * 512],
                        in_=stage[:, :]), waits=[(tv, vz)], inc=(dout, 16))
        SXIF = sc.n(tsm)
        DXNT = sc.n(dout)
        for tc in range(TH // 128):
            for dc in range(8):
                tt = A("tensor", lambda h, tc=tc, dc=dc: h.transpose(
                    ps_t, xif[:, dc * TH + tc * 128:dc * TH + (tc + 1) * 128],
                    idt[:, :]), waits=[(tsm, SXIF), (tv, sc.n(tv))], inc=(ttm, 1))
                A("vector", lambda h, tc=tc, dc=dc: h.tensor_copy(
                    out=xit[:, tc * 1024 + dc * 128:tc * 1024 + (dc + 1) * 128],
                    in_=ps_t), waits=[(ttm, tt), (dout, DXNT)], inc=(tv, 1))
        VXIT = sc.n(tv)
        for nj in range(2):
            def mmp(h, nj=nj):
                last = None
                for dc in range(8):
                    last = h.matmul(ps_a[:, 0:512], wxp[:, dc * 128:(dc + 1) * 128],
                                    xif[:, dc * TH + nj * 512:dc * TH + (nj + 1) * 512],
                                    start=(dc == 0), stop=(dc == 7))
                return last
            tmp_ = A("tensor", mmp, waits=[(tsm, SXIF), (tv, sc.n(tv))], inc=(ttm, 1))
            A("vector", lambda h, nj=nj: h.tensor_copy(
                out=xp64[:, nj * 512:(nj + 1) * 512], in_=ps_a[0:64, 0:512]),
                waits=[(ttm, tmp_)], inc=(tv, 1))
            vbc = A("vector", lambda h, nj=nj: h.tensor_copy(
                out=stage[0:64, :], in_=ps_a[64:128, 0:512]),
                waits=[(dout, sc.n(dout))], inc=(tv, 1))
            A("sync", lambda h, nj=nj, t0=t0: h.dma_start(
                out=bcF[:, t0 + nj * 512:t0 + (nj + 1) * 512], in_=stage[0:64, :]),
                waits=[(tv, vbc)], inc=(dout, 16))
        for tc in range(TH // 128):
            for jc2 in range(2):
                tdt = A("tensor", lambda h, tc=tc, jc2=jc2: h.matmul(
                    ps_b[:, 0:512], xp64[:, tc * 128:(tc + 1) * 128],
                    wdt[:, jc2 * 512:(jc2 + 1) * 512], start=True, stop=True),
                    waits=[(tv, sc.n(tv))], inc=(ttm, 1))
                A("vector", lambda h, jc2=jc2: h.tensor_tensor(
                    out=dtile[:, :], in0=ps_b[:, 0:512],
                    in1=bdtb[:, jc2 * 512:(jc2 + 1) * 512], op=ALU.add),
                    waits=[(ttm, tdt), (dout, sc.n(dout))], inc=(tv, 1))
                sab = A("scalar", lambda h: h.activation(
                    out=atile[:, :], in_=dtile[:, :], func=AF.Abs),
                    waits=[(tv, sc.n(tv))])
                A("scalar", lambda h: h.activation(
                    out=atile[:, :], in_=atile[:, :], func=AF.Exp, scale=-1.0))
                sln = A("scalar", lambda h: h.activation(
                    out=atile[:, :], in_=atile[:, :], func=AF.Ln, bias=c1b[:, :]),
                    inc=(tsm, 1))
                A("vector", lambda h: h.tensor_scalar(
                    out=btile[:, :], in0=dtile[:, :], scalar1=0.0, scalar2=None,
                    op0=ALU.max), waits=[(tsm, sln), (dout, sc.n(dout))])
                vdt = A("vector", lambda h: h.tensor_tensor(
                    out=btile[:, :], in0=btile[:, :], in1=atile[:, :], op=ALU.add),
                    inc=(tv, 1))
                A("sync", lambda h, tc=tc, jc2=jc2, t0=t0: h.dma_start(
                    out=dtT[t0 + tc * 128:t0 + (tc + 1) * 128,
                            jc2 * 512:(jc2 + 1) * 512], in_=btile[:, :]),
                    waits=[(tv, vdt)], inc=(dout, 16))
                vdx = A("vector", lambda h, tc=tc, jc2=jc2: h.tensor_tensor(
                    out=dtile[:, :], in0=btile[:, :],
                    in1=xit[:, tc * 1024 + jc2 * 512:tc * 1024 + (jc2 + 1) * 512],
                    op=ALU.mult), inc=(tv, 1))
                A("sync", lambda h, tc=tc, jc2=jc2, t0=t0: h.dma_start(
                    out=dtxT[t0 + tc * 128:t0 + (tc + 1) * 128,
                             jc2 * 512:(jc2 + 1) * 512], in_=dtile[:, :]),
                    waits=[(tv, vdx)], inc=(dout, 16))
        for (wr_, dst) in ((wrr, xresr), (wri, xresi)):
            for nj in range(2):
                def mmr(h, wr_=wr_, nj=nj):
                    last = None
                    for dc in range(8):
                        last = h.matmul(ps_x, wr_[:, dc * 16:(dc + 1) * 16],
                                        xif[:, dc * TH + nj * 512:dc * TH + (nj + 1) * 512],
                                        start=(dc == 0), stop=(dc == 7))
                    return last
                tr = A("tensor", mmr, waits=[(tv, sc.n(tv))], inc=(ttm, 1))
                A("vector", lambda h, dst=dst, nj=nj: h.tensor_copy(
                    out=dst[:, nj * 512:(nj + 1) * 512], in_=ps_x),
                    waits=[(ttm, tr)], inc=(tv, 1))
        A("vector", lambda h: h.tensor_scalar(
            out=hr_[:, :], in0=xresr[:, :], scalar1=1e-10, scalar2=None, op0=ALU.add))
        A("vector", lambda h: h.tensor_tensor(
            out=sqr_[:, :], in0=hr_[:, :], in1=hr_[:, :], op=ALU.mult))
        A("vector", lambda h: h.tensor_tensor(
            out=sqi_[:, :], in0=xresi[:, :], in1=xresi[:, :], op=ALU.mult))
        vhy = A("vector", lambda h: h.tensor_tensor(
            out=sqr_[:, :], in0=sqr_[:, :], in1=sqi_[:, :], op=ALU.add), inc=(tv, 1))
        A("scalar", lambda h: h.activation(out=sqi_[:, :], in_=sqr_[:, :], func=AF.Ln),
          waits=[(tv, vhy)])
        shy = A("scalar", lambda h: h.activation(
            out=irc_[:, :], in_=sqi_[:, :], func=AF.Exp, scale=-0.5), inc=(tsm, 1))
        vur = A("vector", lambda h: h.tensor_tensor(
            out=sqr_[:, :], in0=hr_[:, :], in1=irc_[:, :], op=ALU.mult),
            waits=[(tsm, shy)], inc=(tv, 1))
        A("sync", lambda h, t0=t0: h.dma_start(
            out=uxrF[:, t0:t0 + TH], in_=sqr_[:, :]), waits=[(tv, vur)], inc=(dout, 16))
        vui = A("vector", lambda h: h.tensor_tensor(
            out=sqi_[:, :], in0=xresi[:, :], in1=irc_[:, :], op=ALU.mult), inc=(tv, 1))
        A("sync", lambda h, t0=t0: h.dma_start(
            out=uxiF[:, t0:t0 + TH], in_=sqi_[:, :]), waits=[(tv, vui)], inc=(dout, 16))
        for tc in range(TH // 128):
            def mml(h, tc=tc):
                last = None
                for dc in range(8):
                    last = h.matmul(ps_l, xnf[:, dc * TH + tc * 128:dc * TH + (tc + 1) * 128],
                                    gt[:, dc * 8:(dc + 1) * 8],
                                    start=(dc == 0), stop=(dc == 7))
                return last
            tl = A("tensor", mml, waits=[(tv, sc.n(tv))], inc=(ttm, 1))
            A("vector", lambda h: h.tensor_reduce(
                out=smx[:, :], in_=ps_l, axis=AX.X, op=ALU.max),
                waits=[(ttm, tl)])
            vsm = A("vector", lambda h: h.tensor_tensor(
                out=sprob[:, :], in0=ps_l, in1=smx[:, :].broadcast_to((128, 8)),
                op=ALU.subtract), waits=[(dout, sc.n(dout))], inc=(tv, 1))
            sex = A("scalar", lambda h: h.activation(
                out=sprob[:, :], in_=sprob[:, :], func=AF.Exp),
                waits=[(tv, vsm)], inc=(tsm, 1))
            vss = A("vector", lambda h: h.tensor_reduce(
                out=ssum[:, :], in_=sprob[:, :], axis=AX.X, op=ALU.add),
                waits=[(tsm, sex)], inc=(tv, 1))
            A("scalar", lambda h: h.activation(out=smx[:, :], in_=ssum[:, :], func=AF.Ln),
              waits=[(tv, vss)])
            sre = A("scalar", lambda h: h.activation(out=sinv[:, :], in_=smx[:, :],
                                                     func=AF.Exp, scale=-1.0), inc=(tsm, 1))
            vpr = A("vector", lambda h: h.tensor_tensor(
                out=sprob[:, :], in0=sprob[:, :],
                in1=sinv[:, :].broadcast_to((128, 8)), op=ALU.mult),
                waits=[(tsm, sre)], inc=(tv, 1))
            A("sync", lambda h, tc=tc, t0=t0: h.dma_start(
                out=probsT[t0 + tc * 128:t0 + (tc + 1) * 128, :], in_=sprob[:, :]),
                waits=[(tv, vpr)], inc=(dout, 16))

    A("sync", lambda h: h.nop(), waits=[(dout, sc.n(dout))])

    blk = nc.Block()
    ctx.append(blk)
    block = blk.__enter__()
    sc.emit(block)
    return ctx


def pre_host_inputs(ins, core):
    b, q = core // 4, core % 4
    x = ins["L0_attn_x"][b, q * TPC:(q + 1) * TPC, :]
    W_res = ins["L0_attn_W_res"].reshape(D, 16, 2)
    return {
        "xT": np.ascontiguousarray(x.T),
        "W_in": ins["L0_attn_W_in"],
        "W_xp": ins["L0_attn_W_xproj"],
        "W_dt": ins["L0_attn_W_dt"],
        "b_dt": ins["L0_attn_b_dt"].reshape(1, D),
        "Wres_r": np.ascontiguousarray(W_res[:, :, 0]),
        "Wres_i": np.ascontiguousarray(W_res[:, :, 1]),
        "gate": ins["L0_ffn_gate"],
        "ident": np.eye(128, dtype=np.float32),
    }



def build_scan2(nc, S_steps=S):
    ctx = []
    sb = lambda n_, s_, d_=F32: _mk(nc, ctx, "sb", n_, s_, d_)
    ps = lambda n_, s_: _mk(nc, ctx, "ps", n_, s_)
    NCHUNK = S_steps // CH
    SB_ = 64
    NB = S_steps // SB_
    LM0 = float(np.log(1e-4 + 1e-8))
    VI, SI, TI = 5, 4, 3

    dtT_b = nc.dram_tensor("dtT_b", [S_steps, D], F32, kind="ExternalInput").ap()
    dtxT_b = nc.dram_tensor("dtxT_b", [S_steps, D], F32, kind="ExternalInput").ap()
    Bsl = nc.dram_tensor("Bsl", [8, S_steps], F32, kind="ExternalInput").ap()
    Csl = nc.dram_tensor("Csl", [8, S_steps], F32, kind="ExternalInput").ap()
    Uxsl = nc.dram_tensor("Uxsl", [8, S_steps], F32, kind="ExternalInput").ap()
    A4 = nc.dram_tensor("A4", [4, 1], F32, kind="ExternalInput").ap()
    h0c = nc.dram_tensor("h0c", [128, 64], F32, kind="ExternalInput").ap()
    sel36I = nc.dram_tensor("sel36", [128, 36], F32, kind="ExternalInput").ap()
    rep4I = nc.dram_tensor("rep4", [4, 64], F32, kind="ExternalInput").ap()
    repbI = nc.dram_tensor("repb", [8, 128], F32, kind="ExternalInput").ap()
    repbnI = nc.dram_tensor("repbn", [8, 128], F32, kind="ExternalInput").ap()
    maskcI = nc.dram_tensor("maskc", [128, 16], F32, kind="ExternalInput").ap()
    umaskI = nc.dram_tensor("umask", [128, 4], F32, kind="ExternalInput").ap()
    yout = nc.dram_tensor("y", [16, 64, S_steps], F32, kind="ExternalOutput").ap()

    Pst = nc.dram_tensor("Pst_i", [64, S_steps, 64], F32, kind="Internal").ap()
    Bst = nc.dram_tensor("Bst_i", [128, S_steps, 64], F32, kind="Internal").ap()
    Cst = nc.dram_tensor("Cst_i", [128, S_steps, 16], F32, kind="Internal").ap()
    Ust = nc.dram_tensor("Ust_i", [128, S_steps, 4], F32, kind="Internal").ap()

    dtb4 = sb("dtb4", [64, SB_ * 64])
    alph = sb("alph", [64, SB_ * 64])
    dxb8 = sb("dxb8", [128, SB_ * 64])
    bstb = sb("bstb", [128, SB_ * 64])
    cstb = sb("cstb", [128, SB_ * 16])
    ustb = sb("ustb", [128, SB_ * 4])
    bslb = sb("bslb", [8, SB_])
    bvalb = sb("bvalb", [128, SB_])
    avec = sb("avec", [64, 1])
    a4t = sb("a4t", [4, 1])
    repb_t = sb("repb_t", [8, 128])
    repbn_t = sb("repbn_t", [8, 128])
    maskc_t = sb("maskc_t", [128, 16])
    umask_t = sb("umask_t", [128, 4])
    ps_bb = ps("ps_bb", [128, SB_])

    Pb = sb("Pbuf", [64, 2 * CH * 64])
    Bb = sb("Bbuf", [128, 2 * CH * 64])
    Cb = sb("Cbuf", [128, 2 * CH * 16])
    Ub = sb("Ubuf", [128, 2 * CH * 4])
    h = sb("h", [128, 64])
    Lm = sb("Lm", [64, 64])
    hred = sb("hred", [128, 1])
    w = sb("w", [64, 64])
    u2 = sb("u2", [64, 64])
    hp = sb("hp", [128, 64])
    sq3 = sb("sq3", [64, 128])
    bsum = sb("bsum", [64, 64])
    dd = sb("dd", [64, 64])
    rr = sb("rr", [64, 64])
    av = sb("av", [64, 64])
    scd = sb("scd", [128, 64])
    th = sb("th", [64, 64])
    sq36 = sb("sq36", [36, 1])
    ird = sb("ird", [4, 1])
    t4s = sb("t4", [4, 1])
    halfc = sb("half", [4, 1])
    eps64 = sb("eps64", [64, 1])
    selu = sb("selu", [128, 68])
    cfix = sb("cfix", [128, 16])
    rept = sb("rep", [4, 64])
    yb = sb("yb", [16, 2 * CH * 64])
    ybh = yb

    ps_e1 = ps("ps_e1", [128, 64])
    ps_m = ps("ps_m2", [64, 128])
    ps_z = ps("ps_z", [64, 128])
    ps_th = ps("ps_th", [64, 64])
    ps_c = ps("ps_c", [128, 8])
    ps_tv = ps("ps_tv", [64, 1])
    ps_y = ps("ps_y", [16, 64])

    ld = _sem(nc, ctx, "s_ld")
    ic = _sem(nc, ctx, "s_ic")
    sv = _sem(nc, ctx, "s_sv")
    ss = _sem(nc, ctx, "s_ss")
    st_ = _sem(nc, ctx, "s_st")
    yc = _sem(nc, ctx, "s_yc")
    yw = _sem(nc, ctx, "s_yw")
    bl = _sem(nc, ctx, "s_bl")
    bv = _sem(nc, ctx, "s_bv")
    bs = _sem(nc, ctx, "s_bs")
    bt = _sem(nc, ctx, "s_bt")
    bw = _sem(nc, ctx, "s_bw")

    sc = Script()
    A = sc.add

    A("sync", lambda h_: h_.dma_start(out=a4t[:, :], in_=A4), inc=(bl, 16))
    A("sync", lambda h_: h_.dma_start(out=repb_t[:, :], in_=repbI), inc=(bl, 16))
    A("sync", lambda h_: h_.dma_start(out=repbn_t[:, :], in_=repbnI), inc=(bl, 16))
    A("sync", lambda h_: h_.dma_start(out=maskc_t[:, :], in_=maskcI), inc=(bl, 16))
    A("sync", lambda h_: h_.dma_start(out=umask_t[:, :], in_=umaskI), inc=(bl, 16))
    A("sync", lambda h_: h_.dma_start(out=rept[:, :], in_=rep4I), inc=(bl, 16))
    LDA = sc.n(bl)
    s1 = A("scalar", lambda h_: h_.activation(out=a4t[:, :], in_=a4t[:, :], func=AF.Exp),
           waits=[(bl, LDA)], inc=(bs, 1))
    v1 = A("vector", lambda h_: h_.tensor_scalar(out=a4t[:, :], in0=a4t[:, :],
                                                 scalar1=-1.0, scalar2=None,
                                                 op0=ALU.mult),
           waits=[(bs, s1)], inc=(bv, 1))
    t1_ = A("tensor", lambda h_: h_.matmul(ps_bb[0:64, 0:1], rept[:, :], a4t[:, :],
                                           start=True, stop=True),
            waits=[(bv, v1)], inc=(bt, 1))
    v2 = A("vector", lambda h_: h_.tensor_copy(out=avec[:, :], in_=ps_bb[0:64, 0:1]),
           waits=[(bt, t1_)], inc=(bv, 1))

    for blk_i in range(NB):
        r0 = blk_i * SB_
        r1 = r0 + SB_
        for q in range(4):
            A("sync", lambda h_, q=q, r0=r0, r1=r1: h_.dma_start(
                out=dtb4[q * 16:(q + 1) * 16, :].rearrange("p (s f) -> p s f", f=64),
                in_=dtT_b[r0:r1, :].rearrange("s (k f) -> k s f", f=64)),
                waits=[(bv, sc.n(bv))] if q == 0 else (), inc=(bl, 16))
        sp = A("scalar", lambda h_: h_.activation(out=alph[:, :], in_=dtb4[:, :],
                                                  func=AF.Exp, scale=avec[:, :]),
               waits=[(bl, sc.n(bl)), (bv, sc.n(bv))], inc=(bs, 1))
        vp = A("vector", lambda h_: h_.tensor_scalar(
            out=alph[:, :], in0=alph[:, :], scalar1=-0.5 * KAPPA,
            scalar2=0.5 * KAPPA, op0=ALU.mult, op1=ALU.add),
            waits=[(bs, sp), (bw, sc.n(bw))], inc=(bv, 1))
        A("sync", lambda h_, r0=r0, r1=r1: h_.dma_start(
            out=Pst[:, r0:r1, :].rearrange("p s f -> p (s f)"), in_=alph[:, :]),
            waits=[(bv, vp)], inc=(bw, 16))
        for q in range(8):
            A("sync", lambda h_, q=q, r0=r0, r1=r1: h_.dma_start(
                out=dxb8[q * 16:(q + 1) * 16, :].rearrange("p (s f) -> p s f", f=64),
                in_=dtxT_b[r0:r1, :].rearrange("s (k f) -> k s f", f=64)),
                inc=(bl, 16))
        A("sync", lambda h_, r0=r0, r1=r1: h_.dma_start(
            out=bslb[:, :], in_=Bsl[:, r0:r1]), inc=(bl, 16))
        tb = A("tensor", lambda h_: h_.matmul(ps_bb[:, :], repb_t[:, :], bslb[:, :],
                                              start=True, stop=True),
               waits=[(bl, sc.n(bl)), (bv, sc.n(bv))], inc=(bt, 1))
        vb0 = A("vector", lambda h_: h_.tensor_copy(out=bvalb[:, :], in_=ps_bb[:, :]),
                waits=[(bt, tb)], inc=(bv, 1))
        vb = A("vector", lambda h_: h_.tensor_tensor(
            out=bstb[:, :].rearrange("p (s f) -> p s f", f=64),
            in0=dxb8[:, :].rearrange("p (s f) -> p s f", f=64),
            in1=bvalb[:, :].unsqueeze(2).broadcast_to((128, SB_, 64)),
            op=ALU.mult), waits=[(bw, sc.n(bw))], inc=(bv, 1))
        A("sync", lambda h_, r0=r0, r1=r1: h_.dma_start(
            out=Bst[:, r0:r1, :].rearrange("p s f -> p (s f)"), in_=bstb[:, :]),
            waits=[(bv, vb)], inc=(bw, 16))
        A("sync", lambda h_, r0=r0, r1=r1: h_.dma_start(
            out=bslb[:, :], in_=Csl[:, r0:r1]), waits=[(bt, sc.n(bt))], inc=(bl, 16))
        tc_ = A("tensor", lambda h_: h_.matmul(ps_bb[:, :], repb_t[:, :], bslb[:, :],
                                               start=True, stop=True),
                waits=[(bl, sc.n(bl)), (bv, sc.n(bv))], inc=(bt, 1))
        vc0 = A("vector", lambda h_: h_.tensor_copy(out=bvalb[:, :], in_=ps_bb[:, :]),
                waits=[(bt, tc_)], inc=(bv, 1))
        vc = A("vector", lambda h_: h_.tensor_tensor(
            out=cstb[:, :].rearrange("p (s f) -> p s f", f=16),
            in0=bvalb[:, :].unsqueeze(2).broadcast_to((128, SB_, 16)),
            in1=maskc_t[:, :].unsqueeze(1).broadcast_to((128, SB_, 16)),
            op=ALU.mult), waits=[(bw, sc.n(bw))], inc=(bv, 1))
        A("sync", lambda h_, r0=r0, r1=r1: h_.dma_start(
            out=Cst[:, r0:r1, :].rearrange("p s f -> p (s f)"), in_=cstb[:, :]),
            waits=[(bv, vc)], inc=(bw, 16))
        A("sync", lambda h_, r0=r0, r1=r1: h_.dma_start(
            out=bslb[:, :], in_=Uxsl[:, r0:r1]), waits=[(bt, sc.n(bt))], inc=(bl, 16))
        tu = A("tensor", lambda h_: h_.matmul(ps_bb[:, :], repbn_t[:, :], bslb[:, :],
                                              start=True, stop=True),
               waits=[(bl, sc.n(bl)), (bv, sc.n(bv))], inc=(bt, 1))
        vu0 = A("vector", lambda h_: h_.tensor_copy(out=bvalb[:, :], in_=ps_bb[:, :]),
                waits=[(bt, tu)], inc=(bv, 1))
        vu = A("vector", lambda h_: h_.tensor_tensor(
            out=ustb[:, :].rearrange("p (s f) -> p s f", f=4),
            in0=bvalb[:, :].unsqueeze(2).broadcast_to((128, SB_, 4)),
            in1=umask_t[:, :].unsqueeze(1).broadcast_to((128, SB_, 4)),
            op=ALU.mult), waits=[(bw, sc.n(bw))], inc=(bv, 1))
        A("sync", lambda h_, r0=r0, r1=r1: h_.dma_start(
            out=Ust[:, r0:r1, :].rearrange("p s f -> p (s f)"), in_=ustb[:, :]),
            waits=[(bv, vu)], inc=(bw, 16))
    BWALL = sc.n(bw)
    BVALL = sc.n(bv)
    BSALL = sc.n(bs)
    BTALL = sc.n(bt)

    def sync_body(h_):
        h_.wait_ge(bw, BWALL)
        h_.dma_start(out=h[:, :], in_=h0c).then_inc(ic, 16)
        h_.dma_start(out=selu[:, 0:36], in_=sel36I).then_inc(ic, 16)
        for k in range(NCHUNK):
            if k >= 2:
                h_.wait_ge(ss, SI * CH * (k - 1))
                h_.wait_ge(st_, TI * CH * (k - 1))
            par = k % 2
            r0, r1 = k * CH, (k + 1) * CH
            h_.dma_start(out=Pb[:, par * CH * 64:(par + 1) * CH * 64],
                         in_=Pst[:, r0:r1, :].rearrange("p s f -> p (s f)")).then_inc(ld, 16)
            h_.dma_start(out=Bb[:, par * CH * 64:(par + 1) * CH * 64],
                         in_=Bst[:, r0:r1, :].rearrange("p s f -> p (s f)")).then_inc(ld, 16)
            h_.dma_start(out=Cb[:, par * CH * 16:(par + 1) * CH * 16],
                         in_=Cst[:, r0:r1, :].rearrange("p s f -> p (s f)")).then_inc(ld, 16)
            h_.dma_start(out=Ub[:, par * CH * 4:(par + 1) * CH * 4],
                         in_=Ust[:, r0:r1, :].rearrange("p s f -> p (s f)")).then_inc(ld, 16)
            h_.wait_ge(yc, (k + 1) * CH)
            h_.dma_start(out=yout[:, :, r0:r1],
                         in_=yb[:, par * CH * 64:(par + 1) * CH * 64].rearrange(
                             "p (d s) -> p d s", d=64)).then_inc(yw, 16)
        h_.sem_inc(ld, 64)
        h_.wait_ge(yw, 16 * NCHUNK)

    def vector_body(v):
        v.wait_ge(ic, 32)
        v.memset(Lm[:, :], LM0)
        v.memset(halfc[:, :], 0.5)
        v.memset(eps64[:, :], 1e-8)
        v.memset(ps_tv[:, :], 0.5)
        v.memset(ps_e1[:, :], 1.0)
        v.memset(ps_z[:, :], 0.5)
        v.memset(yb[:, :], 0.0)
        v.memset(t4s[:, :], 0.5)
        v.tensor_reduce(out=hred[:, :], in_=h[:, :], axis=AX.X, op=ALU.add)
        v.wait_ge(ld, 64)
        v.memset(selu[:, 36:64], 0.0)
        v.tensor_copy(out=selu[:, 64:68], in_=Ub[:, 0:4])
        v.tensor_copy(out=cfix[:, :], in_=Cb[:, 0:16]).then_inc(sv, 1)
        CMASK = min(2 * CH, S_steps) - 1
        with v.Fori(0, S_steps) as g:
            off = (g & 127) * 64
            Ps = Pb[:, bass.ds(off, 64)]
            Bs = Bb[:, bass.ds(off, 64)]
            v.wait_ge(ld, (g & -64) + 64)
            v.wait_ge(ss, g * SI)
            v.tensor_tensor(out=w[:, :], in0=Lm[:, :], in1=Ps, op=ALU.mult)
            v.wait_ge(st_, g * TI + 2)
            v.tensor_tensor(out=u2[:, :], in0=w[:, :],
                            in1=ps_tv[:, :].broadcast_to((64, 64)),
                            op=ALU.mult).then_inc(sv, 1)
            v.wait_ge(ss, g * SI + 2)
            v.tensor_tensor(out=hp[:, :], in0=h[:, :], in1=ps_e1[:, :], op=ALU.mult)
            v.tensor_tensor(out=hp[:, :], in0=hp[:, :], in1=Bs, op=ALU.add)
            v.tensor_tensor(out=sq3[:, 0:64], in0=hp[0:64, :], in1=hp[0:64, :],
                            op=ALU.mult)
            v.tensor_tensor(out=sq3[:, 64:128], in0=hp[64:128, :], in1=hp[64:128, :],
                            op=ALU.mult)
            v.tensor_tensor(out=ps_m[:, 0:64], in0=sq3[:, 0:64], in1=sq3[:, 64:128],
                            op=ALU.add).then_inc(sv, 1)
            v.wait_ge(ss, g * SI + 3)
            v.tensor_scalar(out=bsum[:, :], in0=ps_z[:, 64:128], scalar1=1.0,
                            scalar2=None, op0=ALU.add)
            v.tensor_tensor(out=dd[:, :], in0=bsum[:, :], in1=ps_z[:, 0:64], op=ALU.mult)
            v.reciprocal(out=rr[:, :], in_=dd[:, :])
            v.tensor_scalar(out=av[:, :], in0=ps_z[:, 64:128], scalar1=-1.0,
                            scalar2=1.0, op0=ALU.mult, op1=ALU.add)
            v.tensor_tensor(out=scd[0:64, :], in0=av[:, :], in1=rr[:, :], op=ALU.mult)
            v.tensor_copy(out=scd[64:128, :], in_=scd[0:64, :])
            v.tensor_tensor(out=th[:, :], in0=scd[0:64, :], in1=ps_z[:, 0:64],
                            op=ALU.mult).then_inc(sv, 1)
            v.scalar_tensor_tensor(out=h[0:64, :], in0=hp[0:64, :], scalar=1.0,
                                   in1=scd[0:64, :], op0=ALU.mult, op1=ALU.mult,
                                   accum_out=hred[0:64, :])
            v.scalar_tensor_tensor(out=h[64:128, :], in0=hp[64:128, :], scalar=1.0,
                                   in1=scd[64:128, :], op0=ALU.mult, op1=ALU.mult,
                                   accum_out=hred[64:128, :]).then_inc(sv, 1)
            v.wait_ge(st_, g * TI + 3)
            v.wait_ge(ld, ((g + 1) & -64) + 64)
            v.tensor_copy(out=selu[:, 64:68], in_=Ub[:, bass.ds(((g + 1) & CMASK) * 4, 4)])
            v.tensor_copy(out=cfix[:, :],
                          in_=Cb[:, bass.ds(((g + 1) & CMASK) * 16, 16)]).then_inc(sv, 1)

    def scalar_body(s_):
        s_.wait_ge(ic, 32)
        with s_.Fori(0, S_steps) as g:
            s_.wait_ge(ld, (g & -64) + 64)
            s_.wait_ge(st_, g * TI + 1)
            s_.activation(out=sq36[:, :], in_=ps_c[0:36, 0:1], func=AF.Square)
            s_.activation(out=ps_c[0:4, 4:5], in_=sq36[0:4, :], func=AF.Identity,
                          bias=sq36[32:36, :])
            s_.activation(out=ps_c[0:4, 6:7], in_=ps_c[0:4, 4:5], func=AF.Ln)
            s_.activation(out=ird[:, :], in_=ps_c[0:4, 6:7], func=AF.Exp, scale=-0.5)
            s_.activation(out=t4s[:, :], in_=ps_c[64:68, 0:1], func=AF.Identity,
                          scale=ird[:, :], bias=halfc[:, :]).then_inc(ss, 1)
            s_.wait_ge(sv, g * VI + 2)
            s_.activation(out=ps_e1[0:64, :], in_=u2[:, :], func=AF.Exp)
            s_.activation(out=ps_e1[64:128, :], in_=u2[:, :], func=AF.Exp).then_inc(ss, 1)
            s_.wait_ge(sv, g * VI + 3)
            s_.activation(out=ps_m[:, 64:128], in_=ps_m[:, 0:64], func=AF.Ln,
                          bias=eps64[:, :])
            s_.activation(out=ps_z[:, 0:64], in_=ps_m[:, 64:128], func=AF.Exp, scale=0.5)
            s_.activation(out=ps_z[:, 64:128], in_=ps_z[:, 0:64], func=AF.Exp,
                          scale=-2.0).then_inc(ss, 1)
            s_.wait_ge(sv, g * VI + 4)
            s_.activation(out=ps_th[:, :], in_=th[:, :], func=AF.Square)
            s_.activation(out=Lm[:, :], in_=ps_th[:, :], func=AF.Ln,
                          bias=eps64[:, :]).then_inc(ss, 1)
            s_.wait_ge(st_, g * TI + 3)
            ydst = bass.AP(yb, ((g & 64) * 64) + (g & 63), [[2 * CH * 64, 16], [CH, 64]])
            s_.activation(out=ydst, in_=ps_y[:, :], func=AF.Identity).then_inc(yc, 1)

    def tensor_body(t):
        t.wait_ge(ic, 32)
        with t.Fori(0, S_steps) as g:
            t.wait_ge(sv, g * VI + 1)
            t.matmul(ps_c[0:68, 0:1], selu[:, :], hred[:, :],
                     start=True, stop=True).then_inc(st_, 1)
            t.wait_ge(ss, g * SI + 1)
            t.matmul(ps_tv[:, :], rept[:, :], t4s[:, :], start=True,
                     stop=True).then_inc(st_, 1)
            t.wait_ge(sv, g * VI + 5)
            t.wait_ge(yc, g)
            t.matmul(ps_y[:, :], cfix[:, :], h[:, :],
                     start=True, stop=True).then_inc(st_, 1)

    A("sync", sync_body)
    A("vector", vector_body, waits=[(bv, BVALL)])
    A("scalar", scalar_body, waits=[(bs, BSALL)])
    A("tensor", tensor_body, waits=[(bt, BTALL)])

    blk = nc.Block()
    ctx.append(blk)
    block = blk.__enter__()
    sc.emit(block)
    return ctx


def scan_host_inputs(pre_outs, ins, core, S_steps=S):
    b, cr = core // 4, core % 4
    ns = [cr * 4 + j for j in range(4)]
    qs = [4 * b + q for q in range(4)]
    dtT_b = np.concatenate([pre_outs[q]["dtT"] for q in qs], 0)[:S_steps]
    dtxT_b = np.concatenate([pre_outs[q]["dtxT"] for q in qs], 0)[:S_steps]
    bcF_b = np.concatenate([pre_outs[q]["bcF"] for q in qs], 1)[:, :S_steps]
    uxr_b = np.concatenate([pre_outs[q]["uxrF"] for q in qs], 1)[:, :S_steps]
    uxi_b = np.concatenate([pre_outs[q]["uxiF"] for q in qs], 1)[:, :S_steps]
    Bsl = np.stack([bcF_b[2 * ns[j_ % 4] + (j_ // 4)] for j_ in range(8)], 0)
    Csl = np.stack([bcF_b[32 + 2 * ns[j_ % 4] + (j_ // 4)] for j_ in range(8)], 0)
    Uxsl = np.stack([(uxr_b if j_ < 4 else uxi_b)[ns[j_ % 4]] for j_ in range(8)], 0)
    A4 = ins["L0_attn_A_log"][ns].reshape(4, 1).astype(np.float32)
    phases = np.linspace(0.0, 2.0 * np.pi * (1.0 - 1.0 / N), N)
    h0 = np.zeros((128, 64), np.float32)
    sel36 = np.zeros((128, 36), np.float32)
    for p in range(128):
        c_, scan = p // 64, (p % 64) // 16
        ph = phases[ns[scan]]
        h0[p, :] = 0.01 * (np.cos(ph) if c_ == 0 else np.sin(ph))
        sel36[p, (0 if c_ == 0 else 32) + scan] = 1.0
    rep4 = np.zeros((4, 64), np.float32)
    for p in range(64):
        rep4[p // 16, p] = 1.0
    repb = np.zeros((8, 128), np.float32)
    for p in range(128):
        repb[p // 16, p] = 1.0
    maskc = np.zeros((128, 16), np.float32)
    umask = np.zeros((128, 4), np.float32)
    for p in range(128):
        maskc[p, p % 16] = 1.0
        umask[p, (p % 64) // 16] = 1.0
    return {
        "dtT_b": np.ascontiguousarray(dtT_b),
        "dtxT_b": np.ascontiguousarray(dtxT_b),
        "Bsl": np.ascontiguousarray(Bsl), "Csl": np.ascontiguousarray(Csl),
        "Uxsl": np.ascontiguousarray(Uxsl), "A4": A4, "h0c": h0,
        "sel36": sel36, "rep4": rep4, "repb": repb,
        "repbn": (-0.5 * repb).astype(np.float32), "maskc": maskc, "umask": umask,
    }



MOE_DEBUG = False


def build_moe(nc):
    ctx = []
    sb = lambda n_, s_, d_=F32: _mk(nc, ctx, "sb", n_, s_, d_)
    ps = lambda n_, s_: _mk(nc, ctx, "ps", n_, s_)
    NIT = 40

    pcol = nc.dram_tensor("pcol", [128, 128], F32, kind="ExternalInput").ap()
    xnTf = nc.dram_tensor("xnTf", [TOK, D], F32, kind="ExternalInput").ap()
    xnFs = nc.dram_tensor("xnFs", [D, TPC], F32, kind="ExternalInput").ap()
    W1 = nc.dram_tensor("W1", [D, 4096], F32, kind="ExternalInput").ap()
    b1c = nc.dram_tensor("b1c", [4096, 1], F32, kind="ExternalInput").ap()
    W2 = nc.dram_tensor("W2", [4096, D], F32, kind="ExternalInput").ap()
    b2c = nc.dram_tensor("b2c", [D, 1], F32, kind="ExternalInput").ap()
    Ws1 = nc.dram_tensor("Ws1", [D, 4096], F32, kind="ExternalInput").ap()
    bs1c = nc.dram_tensor("bs1c", [4096, 1], F32, kind="ExternalInput").ap()
    Ws2 = nc.dram_tensor("Ws2", [4096, D], F32, kind="ExternalInput").ap()
    bs2c = nc.dram_tensor("bs2c", [D, 1], F32, kind="ExternalInput").ap()
    ident = nc.dram_tensor("ident", [128, 128], F32, kind="ExternalInput").ap()

    R_e = nc.dram_tensor("R_e", [TOK, D], F32, kind="ExternalOutput").ap()
    sharedF = nc.dram_tensor("sharedF", [D, TPC], F32, kind="ExternalOutput").ap()

    pc = sb("pc", [128, 128])
    lo = sb("lo", [128, 64])
    mid = sb("mid", [128, 64])
    condc = sb("condc", [128, 64])
    junk = sb("junk", [128, 1])
    junkw = sb("junkw", [128, 512])
    maskt = sb("maskt", [128, 128])
    cnt = sb("cnt", [128, 1])
    tot1 = sb("tot1", [1, 1])
    scn = sb("scn", [128, 128])
    rowt = sb("rowt", [1, 128])
    excl = sb("excl", [1, 128])
    slot = sb("slot", [128, 128])
    slotT = sb("slotT", [128, 128])
    slotS = sb("slotS", [128, 128])
    pcT = sb("pcT", [128, 128])
    tokidF = sb("tokidF", [128, 128])
    iotab = sb("iotab", [128, 128])
    iotaI = sb("iotaI", [128, 128], I32)
    movI = sb("movI", [128, 3 * 128])
    Pbuf = sb("Pbuf", [128, 2 * 1024])
    red3 = sb("red3", [128, 3])
    sc16 = sb("sc16", [128, 48])
    sc3 = sb("sc3", [128, 3])
    idxi = sb("idxi", [128, 16], I32)
    scrow_all = sb("scrow_all", [1, 2048])
    zrow = sb("zrow", [128, 1024])
    xgt = sb("xgt", [128, 1024])
    xgf = sb("xgf", [128, 8 * 512])
    h1f = sb("h1f", [128, 32 * 512])
    outf = sb("outf", [128, 8 * 512])
    outt = sb("outt", [128, 4 * 1024])
    wbuf = sb("wbufm", [128, 8 * 128])
    biasb = sb("biasb", [128, 32])
    scb = sb("scb", [128, 512])
    onesc = sb("onescm", [128, 1])
    onesr = sb("onesrm", [1, 128])
    ones1 = sb("ones1m", [1, 1])
    idt = sb("idtm", [128, 128])

    ps_a = ps("mps_a", [128, 512])
    ps_b = ps("mps_b", [128, 512])
    ps_t = ps("mps_t", [128, 128])
    ps_c1 = ps("mps_c1", [1, 128])
    ps_iA = ps("mps_iA", [128, 4])
    ps_iB = ps("mps_iB", [128, 4])

    ld = _sem(nc, ctx, "m_ld")
    tv = _sem(nc, ctx, "m_tv")
    tsm = _sem(nc, ctx, "m_tsm")
    ttm = _sem(nc, ctx, "m_ttm")
    gpx = _sem(nc, ctx, "m_gp")
    gpe = _sem(nc, ctx, "m_ge")
    dout = _sem(nc, ctx, "m_dout")

    sc = Script()
    A = sc.add

    A("sync", lambda h: h.dma_start(out=pc[:, :], in_=pcol), inc=(ld, 16))
    A("sync", lambda h: h.dma_start(out=idt[:, :], in_=ident), inc=(ld, 16))
    LD0 = sc.n(ld)
    A("vector", lambda h: h.memset(onesc[:, :], 1.0))
    A("vector", lambda h: h.memset(onesr[:, :], 1.0))
    A("vector", lambda h: h.memset(ones1[:, :], 1.0))
    A("vector", lambda h: h.memset(lo[:, :], 0.0))
    A("vector", lambda h: h.memset(zrow[:, :], 0.0))
    A("vector", lambda h: h.memset(xgt[:, :], 0.0), inc=(tv, 1))
    A("gpsimd", lambda h: h.iota(out=iotaI[:, :], pattern=[[128, 128]], base=0,
                                 channel_multiplier=1), inc=(gpe, 1))
    vtk = A("vector", lambda h: h.tensor_copy(out=tokidF[:, :], in_=iotaI[:, :]),
            waits=[(gpe, 1)], inc=(tv, 1))
    A("gpsimd", lambda h: h.iota(out=iotaI[:, :], pattern=[[1, 128]], base=0,
                                 channel_multiplier=0),
      waits=[(tv, vtk)], inc=(gpe, 1))
    A("vector", lambda h: h.tensor_copy(out=iotab[:, :], in_=iotaI[:, :]),
      waits=[(gpe, 2)], inc=(tv, 1))
    for rr_ in range(TOK // 128):
        A("sync", lambda h, rr_=rr_: h.dma_start(
            out=R_e[rr_ * 128:(rr_ + 1) * 128, :], in_=zrow[:, :]),
            waits=[(tv, 1)] if rr_ == 0 else (), inc=(dout, 16))
    DZERO = sc.n(dout)

    for it in range(NIT):
        ck = float(2.0 ** (-(it + 1)))
        A("vector", lambda h, ck=ck: h.tensor_scalar(
            out=mid[:, :], in0=lo[:, :], scalar1=ck, scalar2=None, op0=ALU.add),
          waits=[(ld, LD0)] if it == 0 else [(ttm, sc.n(ttm))])
        A("vector", lambda h: h.memset(junkw[:, :], 0.0))
        A("vector", lambda h: h.memset(junkw[:, :], 0.0))
        A("vector", lambda h: h.tensor_tensor(
            out=maskt[:, :], in0=pc[:, :], in1=mid[:, 0:1].broadcast_to((128, 128)),
            op=ALU.is_gt))
        A("vector", lambda h: h.tensor_reduce(
            out=cnt[:, :], in_=maskt[:, :], axis=AX.X, op=ALU.add))
        A("vector", lambda h: h.memset(junkw[:, :], 0.0))
        vcnt = A("vector", lambda h: h.memset(junkw[:, :], 0.0), inc=(tv, 1))
        tmm = A("tensor", lambda h: h.matmul(ps_c1[0:1, 0:1], onesc[:, :], cnt[:, :],
                                             start=True, stop=True),
                waits=[(tv, vcnt)], inc=(ttm, 1))
        A("vector", lambda h: h.memset(junkw[:, :], 0.0), waits=[(ttm, tmm)])
        A("vector", lambda h: h.memset(junkw[:, :], 0.0))
        A("vector", lambda h: h.tensor_copy(out=tot1[:, :], in_=ps_c1[0:1, 0:1]))
        A("vector", lambda h: h.memset(junkw[:, :], 0.0))
        vev = A("vector", lambda h: h.memset(junkw[:, :], 0.0), inc=(tv, 1))
        tbc = A("tensor", lambda h: h.matmul(ps_t[:, 0:1], onesr[:, :], tot1[:, :],
                                             start=True, stop=True),
                waits=[(tv, vev)], inc=(ttm, 1))
        A("vector", lambda h: h.memset(junkw[:, :], 0.0), waits=[(ttm, tbc)])
        A("vector", lambda h: h.memset(junkw[:, :], 0.0))
        A("vector", lambda h, ck=ck: h.tensor_scalar(
            out=condc[:, :], in0=ps_t[:, 0:1].broadcast_to((128, 64)),
            scalar1=2048.0, scalar2=ck, op0=ALU.is_ge, op1=ALU.mult))
        A("vector", lambda h: h.memset(junkw[:, :], 0.0))
        A("vector", lambda h: h.memset(junkw[:, :], 0.0))
        A("vector", lambda h: h.tensor_tensor(
            out=lo[:, :], in0=lo[:, :], in1=condc[:, :], op=ALU.add), inc=(tv, 1))

    vms = A("vector", lambda h: h.tensor_tensor(
        out=maskt[:, :], in0=pc[:, :], in1=lo[:, 0:1].broadcast_to((128, 128)),
        op=ALU.is_gt), inc=(tv, 1))
    A("vector", lambda h: h.tensor_tensor_scan(
        out=scn[:, :], data0=maskt[:, :], data1=maskt[:, :], initial=0.0,
        op0=ALU.add, op1=ALU.bypass))
    A("vector", lambda h: h.memset(junkw[:, :], 0.0))
    A("vector", lambda h: h.memset(junkw[:, :], 0.0))
    A("vector", lambda h: h.tensor_copy(out=cnt[:, :], in_=scn[:, 127:128]))
    A("vector", lambda h: h.memset(junkw[:, :], 0.0))
    vrt = A("vector", lambda h: h.memset(junkw[:, :], 0.0), inc=(tv, 1))
    trt = A("tensor", lambda h: h.transpose(ps_c1[0:1, :], cnt[:, :], idt[:, :]),
            waits=[(tv, vrt)], inc=(ttm, 1))
    A("vector", lambda h: h.memset(junkw[:, :], 0.0), waits=[(ttm, trt)])
    A("vector", lambda h: h.memset(junkw[:, :], 0.0))
    vrr = A("vector", lambda h: h.tensor_copy(out=rowt[:, :], in_=ps_c1[0:1, :]),
            inc=(tv, 1))
    A("vector", lambda h: h.memset(junkw[:, :], 0.0))
    A("vector", lambda h: h.memset(junkw[:, :], 0.0))
    A("vector", lambda h: h.tensor_tensor_scan(
        out=excl[:, :], data0=rowt[:, :], data1=rowt[:, :], initial=0.0,
        op0=ALU.add, op1=ALU.bypass))
    A("vector", lambda h: h.memset(junkw[:, :], 0.0))
    A("vector", lambda h: h.memset(junkw[:, :], 0.0))
    A("vector", lambda h: h.tensor_tensor(
        out=excl[:, :], in0=excl[:, :], in1=rowt[:, :], op=ALU.subtract))
    A("vector", lambda h: h.memset(junkw[:, :], 0.0))
    vex = A("vector", lambda h: h.memset(junkw[:, :], 0.0), inc=(tv, 1))
    tex = A("tensor", lambda h: h.matmul(ps_t[:, 0:1], excl[:, :], ones1[:, :],
                                         start=True, stop=True),
            waits=[(tv, vex)], inc=(ttm, 1))
    A("vector", lambda h: h.memset(junkw[:, :], 0.0), waits=[(ttm, tex)])
    A("vector", lambda h: h.memset(junkw[:, :], 0.0))
    A("vector", lambda h: h.tensor_tensor(
        out=slot[:, :], in0=scn[:, :], in1=ps_t[:, 0:1].broadcast_to((128, 128)),
        op=ALU.add))
    A("vector", lambda h: h.tensor_scalar(
        out=slot[:, :], in0=slot[:, :], scalar1=-1000001.0, scalar2=None, op0=ALU.add))
    A("vector", lambda h: h.tensor_tensor(
        out=slot[:, :], in0=slot[:, :], in1=maskt[:, :], op=ALU.mult))
    A("vector", lambda h: h.tensor_scalar(
        out=slot[:, :], in0=slot[:, :], scalar1=1e6, scalar2=None, op0=ALU.add))
    A("vector", lambda h: h.memset(junkw[:, :], 0.0))
    vsl = A("vector", lambda h: h.memset(junkw[:, :], 0.0), inc=(tv, 1))
    t1 = A("tensor", lambda h: h.transpose(ps_t[:, :], slot[:, :], idt[:, :]),
           waits=[(tv, vsl)], inc=(ttm, 1))
    v1 = A("vector", lambda h: h.tensor_copy(out=slotT[:, :], in_=ps_t[:, :]),
           waits=[(ttm, t1)], inc=(tv, 1))
    t2 = A("tensor", lambda h: h.transpose(ps_t[:, :], pc[:, :], idt[:, :]),
           waits=[(tv, v1)], inc=(ttm, 1))
    v2 = A("vector", lambda h: h.tensor_copy(out=pcT[:, :], in_=ps_t[:, :]),
           waits=[(ttm, t2)], inc=(tv, 1))
    A("vector", lambda h: h.memset(
        movI.rearrange("q (a b) -> q a b", b=3)[:, :, 2:3], 1.0))
    A("vector", lambda h: h.tensor_copy(
        out=movI.rearrange("q (a b) -> q a b", b=3)[:, :, 0:1],
        in_=tokidF[:, :].unsqueeze(2)))
    vmv = A("vector", lambda h: h.tensor_copy(
        out=movI.rearrange("q (a b) -> q a b", b=3)[:, :, 1:2],
        in_=pcT[:, :].unsqueeze(2)), inc=(tv, 1))

    for st in range(16):
        vsh = A("vector", lambda h, st=st: h.tensor_scalar(
            out=slotS[:, :], in0=slotT[:, :], scalar1=float(-128 * st), scalar2=None,
            op0=ALU.add),
            waits=[(tv, vmv)] if st == 0 else [(ttm, sc.n(ttm))], inc=(tv, 1))
        A("vector", lambda h: h.memset(junkw[:, :], 0.0))
        A("vector", lambda h: h.memset(junkw[:, :], 0.0))
        for grp in range(16):
            half = grp % 2
            for pq in range(8):
                p = grp * 8 + pq
                A("vector", lambda h, p=p, half=half, pq=pq: h.tensor_tensor(
                    out=Pbuf[:, half * 1024 + pq * 128:half * 1024 + (pq + 1) * 128],
                    in0=slotS[:, p:p + 1].broadcast_to((128, 128)),
                    in1=iotab[:, :], op=ALU.is_equal),
                    waits=([(ttm, sc.n(ttm) - 1)] if grp >= 2 else [(tv, vsh)])
                    if pq == 0 else ())
            vp = A("vector", lambda h: h.memset(junkw[:, :], 0.0), inc=(tv, 1))
            psg = ps_iA if grp % 2 == 0 else ps_iB

            def mmp8(h, grp=grp, half=half, psg=psg):
                last = None
                for pq in range(8):
                    p = grp * 8 + pq
                    last = h.matmul(
                        psg[:, 0:3],
                        Pbuf[:, half * 1024 + pq * 128:half * 1024 + (pq + 1) * 128],
                        movI[:, 3 * p:3 * p + 3],
                        start=(pq == 0), stop=(pq == 7))
                return last
            tp = A("tensor", mmp8, waits=[(tv, sc.n(tv))], inc=(ttm, 1))
            A("vector", lambda h: h.memset(junkw[:, :], 0.0), waits=[(ttm, tp)])
            A("vector", lambda h: h.memset(junkw[:, :], 0.0))
            A("vector", lambda h, grp=grp, psg=psg: h.tensor_copy(
                out=sc16[:, 3 * grp:3 * grp + 3], in_=psg[:, 0:3]),
                inc=(tv, 1))
        vrd = A("vector", lambda h: h.tensor_reduce(
            out=red3[:, :].unsqueeze(2),
            in_=sc16[:, 0:48].rearrange("p (g c) -> p c g", c=3),
            axis=AX.X, op=ALU.add), waits=[(tv, sc.n(tv))])
        A("vector", lambda h: h.memset(junkw[:, :], 0.0))
        A("vector", lambda h: h.memset(junkw[:, :], 0.0))
        vq = A("vector", lambda h: h.tensor_scalar(
            out=sc3[:, 2:3], in0=red3[:, 2:3], scalar1=-1e6, scalar2=1e6,
            op0=ALU.mult, op1=ALU.add))
        A("vector", lambda h: h.memset(junkw[:, :], 0.0))
        A("vector", lambda h: h.memset(junkw[:, :], 0.0))
        A("vector", lambda h: h.tensor_tensor(
            out=sc3[:, 1:2], in0=red3[:, 0:1], in1=sc3[:, 2:3], op=ALU.add))
        A("vector", lambda h: h.tensor_copy(out=sc3[:, 0:1], in_=red3[:, 1:2]))
        A("vector", lambda h: h.memset(junkw[:, :], 0.0))
        A("vector", lambda h, st=st: h.tensor_copy(
            out=idxi[:, st:st + 1], in_=sc3[:, 1:2]))
        A("vector", lambda h: h.memset(junkw[:, :], 0.0))
        vsc = A("vector", lambda h: h.memset(junkw[:, :], 0.0), inc=(tv, 1))
        t3 = A("tensor", lambda h: h.transpose(ps_t[0:3, :], sc3[:, 0:3], idt[:, :]),
               waits=[(tv, vsc)], inc=(ttm, 1))
        A("vector", lambda h: h.memset(junkw[:, :], 0.0), waits=[(ttm, t3)])
        A("vector", lambda h: h.memset(junkw[:, :], 0.0))
        A("vector", lambda h, st=st: h.tensor_copy(
            out=scrow_all[:, st * 128:(st + 1) * 128], in_=ps_t[0:1, :]),
            inc=(tv, 1))
    VCOMP = sc.n(tv)
    TCOMP = sc.n(ttm)
    if MOE_DEBUG:
        d_idx = nc.dram_tensor("d_idx", [128, 16], I32, kind="ExternalOutput").ap()
        d_scr = nc.dram_tensor("d_scr", [1, 2048], F32, kind="ExternalOutput").ap()
        d_lo = nc.dram_tensor("d_lo", [128, 1], F32, kind="ExternalOutput").ap()
        d_slot = nc.dram_tensor("d_slot", [128, 128], F32, kind="ExternalOutput").ap()
        d_slotT = nc.dram_tensor("d_slotT", [128, 128], F32, kind="ExternalOutput").ap()
        d_mov = nc.dram_tensor("d_mov", [128, 384], F32, kind="ExternalOutput").ap()
        d_scn = nc.dram_tensor("d_scn", [128, 128], F32, kind="ExternalOutput").ap()
        d_msk = nc.dram_tensor("d_msk", [128, 128], F32, kind="ExternalOutput").ap()
        d_rwt = nc.dram_tensor("d_rwt", [1, 128], F32, kind="ExternalOutput").ap()
        d_exc = nc.dram_tensor("d_exc", [1, 128], F32, kind="ExternalOutput").ap()
        A("sync", lambda h: h.dma_start(out=d_idx, in_=idxi[:, :]),
          waits=[(tv, VCOMP)], inc=(dout, 16))
        A("sync", lambda h: h.dma_start(out=d_scr, in_=scrow_all[:, :]), inc=(dout, 16))
        A("sync", lambda h: h.dma_start(out=d_lo, in_=lo[:, 0:1]), inc=(dout, 16))
        A("sync", lambda h: h.dma_start(out=d_slot, in_=slot[:, :]), inc=(dout, 16))
        A("sync", lambda h: h.dma_start(out=d_slotT, in_=slotT[:, :]), inc=(dout, 16))
        A("sync", lambda h: h.dma_start(out=d_mov, in_=movI[:, :]), inc=(dout, 16))
        A("sync", lambda h: h.dma_start(out=d_scn, in_=scn[:, :]), inc=(dout, 16))
        A("sync", lambda h: h.dma_start(out=d_msk, in_=maskt[:, :]), inc=(dout, 16))
        A("sync", lambda h: h.dma_start(out=d_rwt, in_=rowt[:, :]), inc=(dout, 16))
        A("sync", lambda h: h.dma_start(out=d_exc, in_=excl[:, :]), inc=(dout, 16))

    for tile in range(4):
        for grp in range(4):
            st = tile * 4 + grp
            gg = A("gpsimd", lambda h, st=st: h.indirect_dma_start(
                out=xgt[:, :], out_offset=None, in_=xnTf,
                in_offset=IndirectOffsetOnAxis(idxi[:, st:st + 1], 0),
                bounds_check=TOK - 1, oob_is_err=False),
                waits=[(tv, VCOMP), (ttm, sc.n(ttm))], inc=(gpx, 16))
            for dc in range(8):
                tt = A("tensor", lambda h, dc=dc: h.transpose(
                    ps_t[:, :], xgt[:, dc * 128:(dc + 1) * 128], idt[:, :]),
                    waits=[(gpx, sc.n(gpx)), (tv, sc.n(tv))], inc=(ttm, 1))
                A("vector", lambda h, dc=dc, grp=grp: h.tensor_copy(
                    out=xgf[:, dc * 512 + grp * 128:dc * 512 + (grp + 1) * 128],
                    in_=ps_t[:, :]), waits=[(ttm, tt)], inc=(tv, 1))
        VGF = sc.n(tv)
        tsc = A("tensor", lambda h, tile=tile: h.matmul(
            ps_b[:, 0:512], onesr[:, :], scrow_all[:, tile * 512:(tile + 1) * 512],
            start=True, stop=True),
            waits=[(tv, sc.n(tv)), (ttm, sc.n(ttm))], inc=(ttm, 1))
        A("vector", lambda h: h.tensor_copy(out=scb[:, :], in_=ps_b[:, 0:512]),
          waits=[(ttm, tsc)], inc=(tv, 1))
        for hc in range(32):
            wl = A("sync", lambda h, hc=hc: h.dma_start(
                out=wbuf[:, :].rearrange("p (a c) -> p a c", a=8),
                in_=W1[:, hc * 128:(hc + 1) * 128].rearrange("(a p) c -> p a c", p=128)),
                waits=[(ttm, sc.n(ttm))], inc=(ld, 16))
            bl_ = A("sync", lambda h, hc=hc: h.dma_start(
                out=biasb[:, 0:1], in_=b1c[hc * 128:(hc + 1) * 128, :]),
                waits=[(tsm, sc.n(tsm))], inc=(ld, 16))

            def mmh(h, hc=hc):
                last = None
                for dc in range(8):
                    last = h.matmul(ps_a[:, 0:512], wbuf[:, dc * 128:(dc + 1) * 128],
                                    xgf[:, dc * 512:(dc + 1) * 512],
                                    start=(dc == 0), stop=(dc == 7))
                return last
            tmh = A("tensor", mmh, waits=[(ld, sc.n(ld)), (tv, VGF), (tsm, sc.n(tsm))],
                    inc=(ttm, 1))
            A("scalar", lambda h, hc=hc: h.activation(
                out=h1f[:, hc * 512:(hc + 1) * 512], in_=ps_a[:, 0:512],
                func=AF.Gelu_apprx_tanh, bias=biasb[:, 0:1]),
                waits=[(ttm, tmh)], inc=(tsm, 1))
        for jc in range(8):
            def emit_out(jc=jc):
                for qq in range(4):
                    wl2 = A("sync", lambda h, jc=jc, qq=qq: h.dma_start(
                        out=wbuf[:, :].rearrange("p (a c) -> p a c", a=8),
                        in_=W2[qq * 1024:(qq + 1) * 1024, jc * 128:(jc + 1) * 128].rearrange(
                            "(a p) c -> p a c", p=128)),
                        waits=[(ttm, sc.n(ttm))], inc=(ld, 16))

                    def mmo(h, jc=jc, qq=qq):
                        last = None
                        for dc in range(8):
                            last = h.matmul(
                                ps_b[:, 0:512], wbuf[:, dc * 128:(dc + 1) * 128],
                                h1f[:, (qq * 8 + dc) * 512:(qq * 8 + dc + 1) * 512],
                                start=(qq == 0 and dc == 0), stop=(qq == 3 and dc == 7))
                        return last
                    A("tensor", mmo, waits=[(ld, sc.n(ld)), (tsm, sc.n(tsm)), (tv, sc.n(tv))],
                      inc=(ttm, 1))
                bl2 = A("sync", lambda h, jc=jc: h.dma_start(
                    out=biasb[:, 1:2], in_=b2c[jc * 128:(jc + 1) * 128, :]),
                    waits=[(tsm, sc.n(tsm))], inc=(ld, 16))
                sby = A("scalar", lambda h, jc=jc: h.activation(
                    out=outf[:, jc * 512:(jc + 1) * 512], in_=ps_b[:, 0:512],
                    func=AF.Identity, bias=biasb[:, 1:2]),
                    waits=[(ttm, sc.n(ttm)), (ld, sc.n(ld))], inc=(tsm, 1))
                A("vector", lambda h, jc=jc: h.tensor_tensor(
                    out=outf[:, jc * 512:(jc + 1) * 512],
                    in0=outf[:, jc * 512:(jc + 1) * 512], in1=scb[:, :], op=ALU.mult),
                    waits=[(tsm, sc.n(tsm))], inc=(tv, 1))
            emit_out()
        for tc4 in range(4):
            for dc in range(8):
                tt = A("tensor", lambda h, tc4=tc4, dc=dc: h.transpose(
                    ps_t[:, :], outf[:, dc * 512 + tc4 * 128:dc * 512 + (tc4 + 1) * 128],
                    idt[:, :]), waits=[(tv, sc.n(tv)), (gpx, sc.n(gpx))], inc=(ttm, 1))
                A("vector", lambda h, tc4=tc4, dc=dc: h.tensor_copy(
                    out=outt[:, tc4 * 1024 + dc * 128:tc4 * 1024 + (dc + 1) * 128],
                    in_=ps_t[:, :]), waits=[(ttm, tt)], inc=(tv, 1))
        for grp in range(4):
            st = tile * 4 + grp
            A("gpsimd", lambda h, grp=grp, st=st: h.indirect_dma_start(
                out=R_e, out_offset=IndirectOffsetOnAxis(idxi[:, st:st + 1], 0),
                in_=outt[:, grp * 1024:(grp + 1) * 1024], in_offset=None,
                bounds_check=TOK - 1, oob_is_err=False),
                waits=[(tv, sc.n(tv)), (dout, DZERO)], inc=(gpx, 16))
    GPALL = sc.n(gpx)

    xsf = xgf
    for tile in range(4):
        for dc in range(8):
            A("sync", lambda h, tile=tile, dc=dc: h.dma_start(
                out=xsf[:, dc * 512:(dc + 1) * 512],
                in_=xnFs[dc * 128:(dc + 1) * 128, tile * 512:(tile + 1) * 512]),
                waits=[(ttm, sc.n(ttm)), (gpx, GPALL)] if dc == 0 else (), inc=(ld, 16))
        LXS = sc.n(ld)
        for hc in range(32):
            wl = A("sync", lambda h, hc=hc: h.dma_start(
                out=wbuf[:, :].rearrange("p (a c) -> p a c", a=8),
                in_=Ws1[:, hc * 128:(hc + 1) * 128].rearrange("(a p) c -> p a c", p=128)),
                waits=[(ttm, sc.n(ttm))], inc=(ld, 16))
            bl_ = A("sync", lambda h, hc=hc: h.dma_start(
                out=biasb[:, 2:3], in_=bs1c[hc * 128:(hc + 1) * 128, :]),
                waits=[(tsm, sc.n(tsm))], inc=(ld, 16))

            def mms(h, hc=hc):
                last = None
                for dc in range(8):
                    last = h.matmul(ps_a[:, 0:512], wbuf[:, dc * 128:(dc + 1) * 128],
                                    xsf[:, dc * 512:(dc + 1) * 512],
                                    start=(dc == 0), stop=(dc == 7))
                return last
            tmh = A("tensor", mms, waits=[(ld, sc.n(ld)), (tsm, sc.n(tsm))], inc=(ttm, 1))
            A("scalar", lambda h, hc=hc: h.activation(
                out=h1f[:, hc * 512:(hc + 1) * 512], in_=ps_a[:, 0:512],
                func=AF.Gelu_apprx_tanh, bias=biasb[:, 2:3]),
                waits=[(ttm, tmh)], inc=(tsm, 1))
        for jc in range(8):
            for qq in range(4):
                wl2 = A("sync", lambda h, jc=jc, qq=qq: h.dma_start(
                    out=wbuf[:, :].rearrange("p (a c) -> p a c", a=8),
                    in_=Ws2[qq * 1024:(qq + 1) * 1024, jc * 128:(jc + 1) * 128].rearrange(
                        "(a p) c -> p a c", p=128)),
                    waits=[(ttm, sc.n(ttm))], inc=(ld, 16))

                def mmo(h, jc=jc, qq=qq):
                    last = None
                    for dc in range(8):
                        last = h.matmul(
                            ps_b[:, 0:512], wbuf[:, dc * 128:(dc + 1) * 128],
                            h1f[:, (qq * 8 + dc) * 512:(qq * 8 + dc + 1) * 512],
                            start=(qq == 0 and dc == 0), stop=(qq == 3 and dc == 7))
                    return last
                A("tensor", mmo, waits=[(ld, sc.n(ld)), (tsm, sc.n(tsm)), (tv, sc.n(tv))],
                  inc=(ttm, 1))
            bl2 = A("sync", lambda h, jc=jc: h.dma_start(
                out=biasb[:, 3:4], in_=bs2c[jc * 128:(jc + 1) * 128, :]),
                waits=[(tsm, sc.n(tsm))], inc=(ld, 16))
            sby = A("scalar", lambda h, jc=jc: h.activation(
                out=outf[:, jc * 512:(jc + 1) * 512], in_=ps_b[:, 0:512],
                func=AF.Identity, bias=biasb[:, 3:4]),
                waits=[(ttm, sc.n(ttm)), (ld, sc.n(ld)), (dout, sc.n(dout))], inc=(tsm, 1))
            A("sync", lambda h, tile=tile, jc=jc: h.dma_start(
                out=sharedF[jc * 128:(jc + 1) * 128, tile * 512:(tile + 1) * 512],
                in_=outf[:, jc * 512:(jc + 1) * 512]),
                waits=[(tsm, sc.n(tsm))], inc=(dout, 16))

    A("sync", lambda h: h.nop(), waits=[(dout, sc.n(dout)), (gpx, sc.n(gpx))])

    blk = nc.Block()
    ctx.append(blk)
    block = blk.__enter__()
    sc.emit(block)
    return ctx


def moe_host_inputs(pre_outs, ins, core):
    e = core
    probs_all = np.concatenate([pre_outs[q]["probsT"] for q in range(8)], 0)
    pcol = np.ascontiguousarray(probs_all[:, e].reshape(128, 128))
    xnTf = np.concatenate([pre_outs[q]["xnT"] for q in range(8)], 0)
    return {
        "pcol": pcol, "xnTf": np.ascontiguousarray(xnTf),
        "xnFs": pre_outs[core]["xnF"],
        "W1": np.ascontiguousarray(ins["L0_ffn_W1"][e]),
        "b1c": np.ascontiguousarray(ins["L0_ffn_b1"][e].reshape(4096, 1)),
        "W2": np.ascontiguousarray(ins["L0_ffn_W2"][e]),
        "b2c": np.ascontiguousarray(ins["L0_ffn_b2"][e].reshape(D, 1)),
        "Ws1": ins["L0_ffn_Ws1"],
        "bs1c": np.ascontiguousarray(ins["L0_ffn_bs1"].reshape(4096, 1)),
        "Ws2": ins["L0_ffn_Ws2"],
        "bs2c": np.ascontiguousarray(ins["L0_ffn_bs2"].reshape(D, 1)),
        "ident": np.eye(128, dtype=np.float32),
    }


LAMBDA_INIT = 0.8 - 0.6 * math.exp(-0.3 * 12)
L3_DEBUG = False
SC_ATT = float(64.0 ** -0.25)


def build_l3(nc):
    ctx = []
    sb = lambda n_, s_, d_=F32: _mk(nc, ctx, "sb", n_, s_, d_)
    ps = lambda n_, s_: _mk(nc, ctx, "ps", n_, s_)

    di = lambda n_, s_: nc.dram_tensor(n_, s_, F32, kind="ExternalInput").ap()
    xTs = di("xTs", [D, TPC])
    y0 = di("y0", [D, TPC]); y1 = di("y1", [D, TPC])
    y2 = di("y2", [D, TPC]); y3 = di("y3", [D, TPC])
    zFs = di("zFs", [D, TPC])
    shF = di("shF", [D, TPC])
    Rz = [di(f"R{e}", [TPC, D]) for e in range(8)]
    Wout = di("Wout", [D, D]); bout_c = di("bout_c", [D, 1])
    Wqm = di("Wqm", [D, D]); Wkm = di("Wkm", [D, D])
    Wvm = di("Wvm", [D, D]); Wom = di("Wom", [D, D])
    lq1 = di("lq1", [64, 1]); lk1 = di("lk1", [64, 1])
    lq2 = di("lq2", [64, 1]); lk2 = di("lk2", [64, 1])
    lnw8 = di("lnw8", [1, D]); lnb8 = di("lnb8", [1, D])
    cosD = di("cosD", [128, TPC]); sinD = di("sinD", [128, TPC])
    Jr = di("Jr", [128, 128])
    ident = di("identl", [128, 128])
    WfcT = di("WfcT", [D, 4096]); WprojT = di("WprojT", [4096, D])
    l1w = di("l1w", [4096, 1]); l1b = di("l1b", [4096, 1])

    outF = nc.dram_tensor("outF", [D, TPC], F32, kind="ExternalOutput").ap()
    if L3_DEBUG:
        s1Dbg = nc.dram_tensor("s1Dbg", [D, TPC], F32, kind="ExternalOutput").ap()
        g1Dbg = nc.dram_tensor("g1Dbg", [D, TPC], F32, kind="ExternalOutput").ap()
        aDbg = nc.dram_tensor("aDbg", [D, TPC], F32, kind="ExternalOutput").ap()
        qDbg = nc.dram_tensor("qDbg", [D, TPC], F32, kind="ExternalOutput").ap()
    x1D = nc.dram_tensor("x1D", [D, TPC], F32, kind="Internal").ap()
    xn1D = nc.dram_tensor("xn1D", [D, TPC], F32, kind="Internal").ap()
    WqfD = nc.dram_tensor("WqfD", [D, 4096], F32, kind="Internal").ap()
    WqpD = nc.dram_tensor("WqpD", [4096, D], F32, kind="Internal").ap()

    pool = sb("pool", [128, 16384])
    aFt = pool[:, 0:4096]
    ck_q = pool[:, 4096:5120]
    ck_k = pool[:, 5120:6144]
    ck_v = pool[:, 6144:7168]
    k1d = pool[:, 7168:8192]
    k2d = pool[:, 8192:9216]
    vd = pool[:, 9216:11264]
    at1 = pool[:, 11264:12288]
    at2 = pool[:, 12288:13312]
    tmpw = pool[:, 13312:14336]
    prodt = pool[:, 14336:14848]
    awt1 = pool[:, 14848:14912]
    awt2 = pool[:, 14912:14976]
    sqseg = pool[:, 14976:15040]
    h1f = pool[:, 0:16384]
    u0 = sb("u0", [128, 4096])
    u1 = sb("u1", [128, 4096])
    u2 = sb("u2", [128, 4096])
    u3 = sb("u3", [128, 4096])
    u4 = sb("u4", [128, 4096])
    u5 = sb("u5", [128, 4096])
    xnt = sb("xnt", [128, 4096])
    rbuf = u5[:, 1024:2048]
    ytmp = sb("ytmp", [128, 512])
    cosT = pool[:, 15040:15552]
    sinT = pool[:, 15552:16064]
    wbuf = sb("wbufl", [128, 1024])
    jrt = sb("jrt", [128, 128])
    idt = sb("idtl", [128, 128])
    lnwB = sb("lnwB", [128, 1024])
    lnbB = sb("lnbB", [128, 1024])
    lamc = sb("lamc", [128, 1])
    lcol = sb("lcol", [64, 4])
    sm1 = sb("sm1", [1, 8])
    musb = sb("musb", [1, 512])
    varsb = sb("varsb", [1, 512])
    rstd = sb("rstd", [1, 512])
    muB = sb("muB", [128, 512])
    rstdB = sb("rstdB", [128, 512])
    thq = sb("thq", [128, 2])
    gacc = sb("gacc", [128, 1])
    gtot = sb("gtot", [1, 2])
    onesc = sb("onescl", [128, 1])
    onesr = sb("onesrl", [1, 128])
    ones64 = sb("ones64l", [64, 1])
    eps5t = sb("eps5l", [1, 1])
    junk = sb("junkl", [128, 1])
    junkw = sb("junkwl", [128, 512])
    lrow = sb("lrow", [1, 1024])
    sgm = sb("sgm", [128, 8])

    ps_m = ps("lps_m", [128, 512])
    ps_a = ps("lps_a", [128, 512])
    ps_t = ps("lps_t", [128, 128])
    ps_s1 = ps("lps_s1", [1, 512])
    ps_s2 = ps("lps_s2", [1, 512])
    ps_r = ps("lps_r", [128, 8])

    ld = _sem(nc, ctx, "l_ld")
    tv = _sem(nc, ctx, "l_tv")
    tsc = _sem(nc, ctx, "l_ts")
    ttm = _sem(nc, ctx, "l_tt")
    dout = _sem(nc, ctx, "l_do")

    sc = Script()
    A = sc.add

    def VW(*w):
        return [x_ for x_ in w if x_]

    A("sync", lambda h: h.dma_start(out=idt[:, :], in_=ident), inc=(ld, 16))
    A("sync", lambda h: h.dma_start(out=jrt[:, :], in_=Jr), inc=(ld, 16))
    A("sync", lambda h: h.dma_start(out=lcol[:, 0:1], in_=lq1), inc=(ld, 16))
    A("sync", lambda h: h.dma_start(out=lcol[:, 1:2], in_=lk1), inc=(ld, 16))
    A("sync", lambda h: h.dma_start(out=lcol[:, 2:3], in_=lq2), inc=(ld, 16))
    A("sync", lambda h: h.dma_start(out=lcol[:, 3:4], in_=lk2), inc=(ld, 16))
    A("sync", lambda h: h.dma_start(out=lrow[:, :], in_=lnw8), inc=(ld, 16))
    LDC = sc.n(ld)
    A("vector", lambda h: h.memset(onesc[:, :], 1.0))
    A("vector", lambda h: h.memset(onesr[:, :], 1.0))
    A("vector", lambda h: h.memset(ones64[:, :], 1.0))
    A("vector", lambda h: h.memset(eps5t[:, :], 1e-5), inc=(tv, 1))
    A("vector", lambda h: h.tensor_tensor(out=lcol[:, 0:1], in0=lcol[:, 0:1],
                                          in1=lcol[:, 1:2], op=ALU.mult),
      waits=[(ld, LDC)])
    A("vector", lambda h: h.memset(junkw[:, :], 0.0))
    A("vector", lambda h: h.memset(junkw[:, :], 0.0))
    A("vector", lambda h: h.tensor_tensor(out=lcol[:, 2:3], in0=lcol[:, 2:3],
                                          in1=lcol[:, 3:4], op=ALU.mult))
    A("vector", lambda h: h.memset(junkw[:, :], 0.0))
    vlp = A("vector", lambda h: h.memset(junkw[:, :], 0.0), inc=(tv, 1))
    tl1 = A("tensor", lambda h: h.matmul(ps_t[0:1, 0:4], ones64[:, :],
                                         lcol[0:64, 0:4], start=True, stop=True),
            waits=[(tv, vlp)], inc=(ttm, 1))
    A("vector", lambda h: h.memset(junkw[:, :], 0.0), waits=[(ttm, tl1)])
    vdl = A("vector", lambda h: h.memset(junkw[:, :], 0.0), inc=(tv, 1))
    sl1 = A("scalar", lambda h: h.activation(out=sm1[:, 0:4], in_=ps_t[0:1, 0:4],
                                             func=AF.Exp), waits=[(tv, vdl)], inc=(tsc, 1))
    A("vector", lambda h: h.memset(junkw[:, :], 0.0), waits=[(tsc, sl1)])
    A("vector", lambda h: h.memset(junkw[:, :], 0.0))
    A("vector", lambda h: h.tensor_tensor(out=sm1[:, 4:5], in0=sm1[:, 0:1],
                                          in1=sm1[:, 2:3], op=ALU.subtract))
    A("vector", lambda h: h.memset(junkw[:, :], 0.0))
    A("vector", lambda h: h.memset(junkw[:, :], 0.0))
    A("vector", lambda h: h.tensor_scalar(
        out=sm1[:, 5:6], in0=sm1[:, 4:5], scalar1=float(LAMBDA_INIT), scalar2=-1.0,
        op0=ALU.add, op1=ALU.mult))
    A("vector", lambda h: h.memset(junkw[:, :], 0.0))
    vls = A("vector", lambda h: h.memset(junkw[:, :], 0.0), inc=(tv, 1))
    tl2 = A("tensor", lambda h: h.matmul(ps_t[:, 0:1], onesr[:, :], sm1[:, 5:6],
                                         start=True, stop=True),
            waits=[(tv, vls)], inc=(ttm, 1))
    A("vector", lambda h: h.memset(junkw[:, :], 0.0), waits=[(ttm, tl2)])
    A("vector", lambda h: h.memset(junkw[:, :], 0.0))
    vlc = A("vector", lambda h: h.tensor_copy(out=lamc[:, :], in_=ps_t[:, 0:1]),
            inc=(tv, 1))
    A("vector", lambda h: h.tensor_scalar(
        out=lrow[:, :], in0=lrow[:, :], scalar1=float(1.0 - LAMBDA_INIT), scalar2=None,
        op0=ALU.mult))
    A("vector", lambda h: h.memset(junkw[:, :], 0.0))
    vlw = A("vector", lambda h: h.memset(junkw[:, :], 0.0), inc=(tv, 1))
    tl3 = A("tensor", lambda h: h.matmul(ps_m[:, 0:512], onesr[:, :], lrow[:, 0:512],
                                         start=True, stop=True),
            waits=[(tv, vlw)], inc=(ttm, 1))
    tl4 = A("tensor", lambda h: h.matmul(ps_a[:, 0:512], onesr[:, :], lrow[:, 512:1024],
                                         start=True, stop=True), inc=(ttm, 1))
    A("vector", lambda h: h.tensor_copy(out=lnwB[:, 0:512], in_=ps_m[:, 0:512]),
      waits=[(ttm, tl4)])
    vw1 = A("vector", lambda h: h.tensor_copy(out=lnwB[:, 512:1024], in_=ps_a[:, 0:512]),
            inc=(tv, 1))
    ll2 = A("sync", lambda h: h.dma_start(out=lrow[:, :], in_=lnb8),
            waits=[(tv, vw1)], inc=(ld, 16))
    A("vector", lambda h: h.tensor_scalar(
        out=lrow[:, :], in0=lrow[:, :], scalar1=float(1.0 - LAMBDA_INIT), scalar2=None,
        op0=ALU.mult), waits=[(ld, ll2)])
    A("vector", lambda h: h.memset(junkw[:, :], 0.0))
    vlb = A("vector", lambda h: h.memset(junkw[:, :], 0.0), inc=(tv, 1))
    tl5 = A("tensor", lambda h: h.matmul(ps_m[:, 0:512], onesr[:, :], lrow[:, 0:512],
                                         start=True, stop=True),
            waits=[(tv, vlb)], inc=(ttm, 1))
    tl6 = A("tensor", lambda h: h.matmul(ps_a[:, 0:512], onesr[:, :], lrow[:, 512:1024],
                                         start=True, stop=True), inc=(ttm, 1))
    A("vector", lambda h: h.tensor_copy(out=lnbB[:, 0:512], in_=ps_m[:, 0:512]),
      waits=[(ttm, tl6)])
    vbb = A("vector", lambda h: h.tensor_copy(out=lnbB[:, 512:1024], in_=ps_a[:, 0:512]),
            inc=(tv, 1))

    for tile in range(4):
        c0, c1 = tile * 512, (tile + 1) * 512
        for dc in range(8):
            A("sync", lambda h, dc=dc, c0=c0, c1=c1: h.dma_start(
                out=u0[:, dc * 512:(dc + 1) * 512],
                in_=y0[dc * 128:(dc + 1) * 128, c0:c1]),
                waits=[(tv, sc.n(tv)), (tsc, sc.n(tsc)), (ttm, sc.n(ttm))]
                if dc == 0 else (), inc=(ld, 16))
            A("sync", lambda h, dc=dc, c0=c0, c1=c1: h.dma_start(
                out=u1[:, dc * 512:(dc + 1) * 512],
                in_=zFs[dc * 128:(dc + 1) * 128, c0:c1]), inc=(ld, 16))
            A("sync", lambda h, dc=dc, c0=c0, c1=c1: h.dma_start(
                out=u2[:, dc * 512:(dc + 1) * 512],
                in_=shF[dc * 128:(dc + 1) * 128, c0:c1]), inc=(ld, 16))
            A("sync", lambda h, dc=dc, c0=c0, c1=c1: h.dma_start(
                out=u3[:, dc * 512:(dc + 1) * 512],
                in_=xTs[dc * 128:(dc + 1) * 128, c0:c1]), inc=(ld, 16))
        LD_A = sc.n(ld)
        for dc in range(8):
            for yi, yt in ((1, y1), (2, y2), (3, y3)):
                lyy = A("sync", lambda h, dc=dc, yt=yt, c0=c0, c1=c1: h.dma_start(
                    out=ytmp[:, :], in_=yt[dc * 128:(dc + 1) * 128, c0:c1]),
                    waits=[(tv, sc.n(tv))], inc=(ld, 16))
                A("vector", lambda h, dc=dc: h.tensor_tensor(
                    out=u0[:, dc * 512:(dc + 1) * 512],
                    in0=u0[:, dc * 512:(dc + 1) * 512], in1=ytmp[:, :], op=ALU.add),
                    waits=[(ld, lyy)] + ([(ld, LD_A)] if dc == 0 and yi == 1 else []),
                    inc=(tv, 1))
        for dc in range(8):
            sz = A("scalar", lambda h, dc=dc: h.activation(
                out=u1[:, dc * 512:(dc + 1) * 512], in_=u1[:, dc * 512:(dc + 1) * 512],
                func=AF.Silu), waits=[(ld, LD_A), (tv, sc.n(tv))], inc=(tsc, 1))
            A("vector", lambda h, dc=dc: h.tensor_tensor(
                out=u0[:, dc * 512:(dc + 1) * 512], in0=u0[:, dc * 512:(dc + 1) * 512],
                in1=u1[:, dc * 512:(dc + 1) * 512], op=ALU.mult),
                waits=[(tsc, sz)], inc=(tv, 1))
        for c4 in range(4):
            r0, r1 = c0 + c4 * 128, c0 + (c4 + 1) * 128
            le = A("sync", lambda h, r0=r0, r1=r1: h.dma_start(
                out=rbuf[:, :], in_=Rz[0][r0:r1, :]),
                waits=[(tv, sc.n(tv)), (ttm, sc.n(ttm))], inc=(ld, 16))
            for e in range(1, 8):
                le2 = A("sync", lambda h, e=e, r0=r0, r1=r1: h.dma_start(
                    out=u5[:, 0:1024], in_=Rz[e][r0:r1, :]),
                    waits=[(tv, sc.n(tv))], inc=(ld, 16))
                A("vector", lambda h: h.tensor_tensor(
                    out=rbuf[:, :], in0=rbuf[:, :], in1=u5[:, 0:1024], op=ALU.add),
                    waits=[(ld, le2)] + ([(ld, le)] if e == 1 else []), inc=(tv, 1))
            for dc in range(8):
                tt = A("tensor", lambda h, dc=dc: h.transpose(
                    ps_t[:, :], rbuf[:, dc * 128:(dc + 1) * 128], idt[:, :]),
                    waits=[(tv, sc.n(tv))], inc=(ttm, 1))
                A("vector", lambda h, dc=dc, c4=c4: h.tensor_copy(
                    out=u4[:, dc * 512 + c4 * 128:dc * 512 + (c4 + 1) * 128],
                    in_=ps_t[:, :]),
                    waits=[(ttm, tt)] + ([(dout, sc.n(dout))] if dc == 0 and c4 == 0
                                         else []), inc=(tv, 1))
        for dc in range(8):
            A("vector", lambda h, dc=dc: h.tensor_tensor(
                out=u4[:, dc * 512:(dc + 1) * 512], in0=u4[:, dc * 512:(dc + 1) * 512],
                in1=u2[:, dc * 512:(dc + 1) * 512], op=ALU.add), inc=(tv, 1))
            sg = A("scalar", lambda h, dc=dc: h.activation(
                out=u1[:, dc * 512:(dc + 1) * 512], in_=u4[:, dc * 512:(dc + 1) * 512],
                func=AF.Silu), waits=[(tv, sc.n(tv))], inc=(tsc, 1))
        SGD = sc.n(tsc)
        for jc in range(8):
            wl = A("sync", lambda h, jc=jc: h.dma_start(
                out=wbuf[:, :].rearrange("p (a c) -> p a c", a=8),
                in_=Wout[:, jc * 128:(jc + 1) * 128].rearrange("(a p) c -> p a c", p=128)),
                waits=[(ttm, sc.n(ttm))], inc=(ld, 16))
            bl_ = A("sync", lambda h, jc=jc: h.dma_start(
                out=thq[:, 0:1], in_=bout_c[jc * 128:(jc + 1) * 128, :]),
                waits=[(tsc, sc.n(tsc))], inc=(ld, 16))

            def mmS(h, jc=jc):
                last = None
                for dc in range(8):
                    last = h.matmul(ps_m[:, 0:512], wbuf[:, dc * 128:(dc + 1) * 128],
                                    u0[:, dc * 512:(dc + 1) * 512],
                                    start=(dc == 0), stop=(dc == 7))
                return last
            tms = A("tensor", mmS, waits=[(ld, sc.n(ld)), (tv, sc.n(tv)),
                                          (tsc, sc.n(tsc))], inc=(ttm, 1))
            ss = A("scalar", lambda h, jc=jc: h.activation(
                out=u5[:, jc * 512:(jc + 1) * 512], in_=ps_m[:, 0:512],
                func=AF.Identity, bias=thq[:, 0:1]),
                waits=[(ttm, tms), (ld, sc.n(ld))], inc=(tsc, 1))
            A("vector", lambda h, jc=jc: h.tensor_tensor(
                out=u5[:, jc * 512:(jc + 1) * 512], in0=u5[:, jc * 512:(jc + 1) * 512],
                in1=u1[:, jc * 512:(jc + 1) * 512], op=ALU.mult),
                waits=[(tsc, sc.n(tsc)), (tsc, SGD)], inc=(tv, 1))
            A("vector", lambda h, jc=jc: h.tensor_tensor(
                out=u4[:, jc * 512:(jc + 1) * 512], in0=u3[:, jc * 512:(jc + 1) * 512],
                in1=u5[:, jc * 512:(jc + 1) * 512], op=ALU.add), inc=(tv, 1))
            A("sync", lambda h, jc=jc, c0=c0, c1=c1: h.dma_start(
                out=x1D[jc * 128:(jc + 1) * 128, c0:c1],
                in_=u4[:, jc * 512:(jc + 1) * 512]),
                waits=[(tv, sc.n(tv))], inc=(dout, 16))
        A("vector", lambda h: h.memset(junkw[:, :], 0.0))
        A("vector", lambda h: h.memset(junkw[:, :], 0.0), inc=(tv, 1))

        def mmstats(h):
            last = None
            for dc in range(8):
                last = h.matmul(ps_s1[0:1, 0:512], onesc[:, :],
                                u4[:, dc * 512:(dc + 1) * 512],
                                start=(dc == 0), stop=(dc == 7))
            return last
        tst = A("tensor", mmstats, waits=[(tv, sc.n(tv))], inc=(ttm, 1))
        for dc in range(8):
            sq_ = A("scalar", lambda h, dc=dc: h.activation(
                out=u3[:, dc * 512:(dc + 1) * 512], in_=u4[:, dc * 512:(dc + 1) * 512],
                func=AF.Square), waits=[(ttm, sc.n(ttm))] if dc == 0 else (), inc=(tsc, 1))
        SQD = sc.n(tsc)

        def mmstats2(h):
            last = None
            for dc in range(8):
                last = h.matmul(ps_s2[0:1, 0:512], onesc[:, :],
                                u3[:, dc * 512:(dc + 1) * 512],
                                start=(dc == 0), stop=(dc == 7))
            return last
        ts2 = A("tensor", mmstats2, waits=[(tsc, SQD)], inc=(ttm, 1))
        A("vector", lambda h: h.memset(junkw[:, :], 0.0), waits=[(ttm, ts2)])
        A("vector", lambda h: h.memset(junkw[:, :], 0.0))
        A("vector", lambda h: h.tensor_scalar(
            out=musb[:, :], in0=ps_s1[0:1, 0:512], scalar1=float(1.0 / D), scalar2=None,
            op0=ALU.mult))
        A("vector", lambda h: h.tensor_scalar(
            out=varsb[:, :], in0=ps_s2[0:1, 0:512], scalar1=float(1.0 / D), scalar2=None,
            op0=ALU.mult))
        A("vector", lambda h: h.memset(junkw[:, :], 0.0))
        A("vector", lambda h: h.tensor_tensor(
            out=ytmp[0:1, :], in0=musb[:, :], in1=musb[:, :], op=ALU.mult))
        A("vector", lambda h: h.memset(junkw[:, :], 0.0))
        A("vector", lambda h: h.memset(junkw[:, :], 0.0))
        vvar = A("vector", lambda h: h.tensor_tensor(
            out=varsb[:, :], in0=varsb[:, :], in1=ytmp[0:1, :], op=ALU.subtract),
            inc=(tv, 1))
        sv1 = A("scalar", lambda h: h.activation(
            out=rstd[:, :], in_=varsb[:, :], func=AF.Ln, bias=eps5t[0:1, 0:1]),
            waits=[(tv, vvar)], inc=(tsc, 1))
        sv2 = A("scalar", lambda h: h.activation(
            out=rstd[:, :], in_=rstd[:, :], func=AF.Exp, scale=-0.5),
            inc=(tsc, 1))
        tb1 = A("tensor", lambda h: h.matmul(ps_m[:, 0:512], onesr[:, :], musb[:, :],
                                             start=True, stop=True),
                waits=[(tv, vvar), (tsc, sv2)], inc=(ttm, 1))
        tb2 = A("tensor", lambda h: h.matmul(ps_a[:, 0:512], onesr[:, :], rstd[:, :],
                                             start=True, stop=True), inc=(ttm, 1))
        A("vector", lambda h: h.tensor_copy(out=muB[:, :], in_=ps_m[:, 0:512]),
          waits=[(ttm, tb2)])
        vmu = A("vector", lambda h: h.tensor_copy(out=rstdB[:, :], in_=ps_a[:, 0:512]),
                inc=(tv, 1))
        for dc in range(8):
            A("vector", lambda h, dc=dc: h.tensor_tensor(
                out=u5[:, dc * 512:(dc + 1) * 512], in0=u4[:, dc * 512:(dc + 1) * 512],
                in1=muB[:, :], op=ALU.subtract),
                waits=[(tv, vmu)] if dc == 0 else ())
            A("vector", lambda h, dc=dc: h.tensor_tensor(
                out=u5[:, dc * 512:(dc + 1) * 512], in0=u5[:, dc * 512:(dc + 1) * 512],
                in1=rstdB[:, :], op=ALU.mult), inc=(tv, 1))
            A("sync", lambda h, dc=dc, c0=c0, c1=c1: h.dma_start(
                out=xn1D[dc * 128:(dc + 1) * 128, c0:c1],
                in_=u5[:, dc * 512:(dc + 1) * 512]),
                waits=[(tv, sc.n(tv))], inc=(dout, 16))
    P0_done_ld = sc.n(ld)

    for (wsrc, wdst, nchunk, ncols) in ((WfcT, WqfD, 8, 4096), (WprojT, WqpD, 32, 1024)):
        col = 0 if wsrc is WfcT else 1
        A("vector", lambda h: h.memset(gacc[:, :], 0.0),
          waits=[(ttm, sc.n(ttm))], inc=(tv, 1))
        for cchunk in range(nchunk):
            lw = A("sync", lambda h, cchunk=cchunk, wsrc=wsrc, ncols=ncols: h.dma_start(
                out=u0[:, 0:ncols], in_=wsrc[cchunk * 128:(cchunk + 1) * 128, :]),
                waits=[(tv, sc.n(tv))], inc=(ld, 16))
            A("vector", lambda h, ncols=ncols: h.scalar_tensor_tensor(
                out=u1[:, 0:ncols], in0=u0[:, 0:ncols], scalar=-1.0,
                in1=u0[:, 0:ncols], op0=ALU.mult, op1=ALU.max),
                waits=[(ld, lw)])
            A("vector", lambda h, ncols=ncols: h.tensor_reduce(
                out=ytmp[:, 0:1], in_=u1[:, 0:ncols], axis=AX.X, op=ALU.add))
            A("vector", lambda h: h.memset(junkw[:, :], 0.0))
            A("vector", lambda h: h.memset(junkw[:, :], 0.0))
            A("vector", lambda h: h.tensor_tensor(
                out=gacc[:, :], in0=gacc[:, :], in1=ytmp[:, 0:1], op=ALU.add))
            A("vector", lambda h: h.memset(junkw[:, :], 0.0))
            A("vector", lambda h: h.memset(junkw[:, :], 0.0), inc=(tv, 1))
        tg = A("tensor", lambda h, col=col: h.matmul(
            ps_t[0:1, col:col + 1], onesc[:, :], gacc[:, :], start=True, stop=True),
            waits=[(tv, sc.n(tv))], inc=(ttm, 1))
        A("vector", lambda h: h.memset(junkw[:, :], 0.0), waits=[(ttm, tg)])
        A("vector", lambda h: h.memset(junkw[:, :], 0.0))
        A("vector", lambda h, col=col, nchunk=nchunk, ncols=ncols: h.tensor_scalar(
            out=gtot[:, col:col + 1], in0=ps_t[0:1, col:col + 1],
            scalar1=float(1.0 / (nchunk * 128 * ncols)), scalar2=float(1e-6),
            op0=ALU.mult, op1=ALU.add))
        A("vector", lambda h: h.memset(junkw[:, :], 0.0))
        A("vector", lambda h: h.memset(junkw[:, :], 0.0))
        A("vector", lambda h, col=col: h.tensor_scalar(
            out=gtot[:, col:col + 1], in0=gtot[:, col:col + 1], scalar1=0.5,
            scalar2=None, op0=ALU.mult))
        A("vector", lambda h: h.memset(junkw[:, :], 0.0))
        vth = A("vector", lambda h: h.memset(junkw[:, :], 0.0), inc=(tv, 1))
        tth = A("tensor", lambda h, col=col: h.matmul(
            ps_t[:, 2:3], onesr[:, :], gtot[:, col:col + 1], start=True, stop=True),
            waits=[(tv, vth)], inc=(ttm, 1))
        A("vector", lambda h: h.memset(junkw[:, :], 0.0), waits=[(ttm, tth)])
        A("vector", lambda h: h.memset(junkw[:, :], 0.0))
        A("vector", lambda h: h.tensor_copy(out=thq[:, 0:1], in_=ps_t[:, 2:3]))
        A("vector", lambda h: h.memset(junkw[:, :], 0.0))
        A("vector", lambda h: h.memset(junkw[:, :], 0.0))
        vng = A("vector", lambda h: h.tensor_scalar(
            out=thq[:, 1:2], in0=thq[:, 0:1], scalar1=-1.0, scalar2=None,
            op0=ALU.mult), inc=(tv, 1))
        for cchunk in range(nchunk):
            lw = A("sync", lambda h, cchunk=cchunk, wsrc=wsrc, ncols=ncols: h.dma_start(
                out=u0[:, 0:ncols], in_=wsrc[cchunk * 128:(cchunk + 1) * 128, :]),
                waits=[(tv, sc.n(tv)), (dout, sc.n(dout))], inc=(ld, 16))
            A("vector", lambda h, ncols=ncols: h.tensor_tensor(
                out=u1[:, 0:ncols], in0=u0[:, 0:ncols],
                in1=thq[:, 0:1].broadcast_to((128, ncols)), op=ALU.is_gt),
                waits=[(ld, lw), (tv, vng)])
            A("vector", lambda h, ncols=ncols: h.tensor_tensor(
                out=u2[:, 0:ncols], in0=u0[:, 0:ncols],
                in1=thq[:, 1:2].broadcast_to((128, ncols)), op=ALU.is_lt))
            A("vector", lambda h, ncols=ncols: h.tensor_tensor(
                out=u1[:, 0:ncols], in0=u1[:, 0:ncols], in1=u2[:, 0:ncols],
                op=ALU.subtract), inc=(tv, 1))
            A("sync", lambda h, cchunk=cchunk, wdst=wdst, ncols=ncols: h.dma_start(
                out=wdst[cchunk * 128:(cchunk + 1) * 128, :], in_=u1[:, 0:ncols]),
                waits=[(tv, sc.n(tv))], inc=(dout, 16))
    QT_dout = sc.n(dout)

    for tile in range(4):
        c0, c1 = tile * 512, (tile + 1) * 512
        for dc in range(8):
            A("sync", lambda h, dc=dc, c0=c0, c1=c1: h.dma_start(
                out=xnt[:, dc * 512:(dc + 1) * 512],
                in_=xn1D[dc * 128:(dc + 1) * 128, c0:c1]),
                waits=[(tv, sc.n(tv)), (dout, QT_dout), (tsc, sc.n(tsc)),
                       (ttm, sc.n(ttm))]
                if dc == 0 else (), inc=(ld, 16))
            A("sync", lambda h, dc=dc, c0=c0, c1=c1: h.dma_start(
                out=u3[:, dc * 512:(dc + 1) * 512],
                in_=x1D[dc * 128:(dc + 1) * 128, c0:c1]), inc=(ld, 16))
        A("sync", lambda h, c0=c0, c1=c1: h.dma_start(
            out=cosT[:, :], in_=cosD[:, c0:c1]),
            waits=[(tv, sc.n(tv))], inc=(ld, 16))
        A("sync", lambda h, c0=c0, c1=c1: h.dma_start(
            out=sinT[:, :], in_=sinD[:, c0:c1]), inc=(ld, 16))
        LDX = sc.n(ld)
        for wm, udst in ((Wqm, u0), (Wkm, u1), (Wvm, u2)):
            for jc in range(8):
                wl = A("sync", lambda h, jc=jc, wm=wm: h.dma_start(
                    out=wbuf[:, :].rearrange("p (a c) -> p a c", a=8),
                    in_=wm[:, jc * 128:(jc + 1) * 128].rearrange("(a p) c -> p a c", p=128)),
                    waits=[(ttm, sc.n(ttm))], inc=(ld, 16))

                def mmq(h, jc=jc):
                    last = None
                    for dc in range(8):
                        last = h.matmul(ps_m[:, 0:512], wbuf[:, dc * 128:(dc + 1) * 128],
                                        xnt[:, dc * 512:(dc + 1) * 512],
                                        start=(dc == 0), stop=(dc == 7))
                    return last
                tmq = A("tensor", mmq, waits=[(ld, sc.n(ld)), (tv, sc.n(tv))],
                        inc=(ttm, 1))
                A("vector", lambda h, jc=jc, udst=udst: h.tensor_copy(
                    out=udst[:, jc * 512:(jc + 1) * 512], in_=ps_m[:, 0:512]),
                    waits=[(ttm, tmq)], inc=(tv, 1))
        for udst in (u0, u1):
            for dc in range(8):
                tr = A("tensor", lambda h, dc=dc, udst=udst: h.matmul(
                    ps_a[:, 0:512], jrt[:, :], udst[:, dc * 512:(dc + 1) * 512],
                    start=True, stop=True), waits=[(tv, sc.n(tv))], inc=(ttm, 1))
                A("vector", lambda h, dc=dc, udst=udst: h.tensor_tensor(
                    out=udst[:, dc * 512:(dc + 1) * 512],
                    in0=udst[:, dc * 512:(dc + 1) * 512], in1=cosT[:, :], op=ALU.mult),
                    waits=[(ttm, tr)] + ([(ld, LDX)] if dc == 0 and udst is u0 else []))
                A("vector", lambda h, dc=dc: h.tensor_tensor(
                    out=ytmp[:, :], in0=ps_a[:, 0:512], in1=sinT[:, :], op=ALU.mult),
                    waits=[(ttm, tr)])
                A("vector", lambda h, dc=dc, udst=udst: h.tensor_tensor(
                    out=udst[:, dc * 512:(dc + 1) * 512],
                    in0=udst[:, dc * 512:(dc + 1) * 512], in1=ytmp[:, :], op=ALU.add),
                    inc=(tv, 1))
        ROTD = sc.n(tv)
        for c4 in range(4):
            for usrc, cdst in ((u0, ck_q), (u1, ck_k), (u2, ck_v)):
                for dc in range(8):
                    tt = A("tensor", lambda h, dc=dc, usrc=usrc, c4=c4: h.transpose(
                        ps_t[:, :], usrc[:, dc * 512 + c4 * 128:dc * 512 + (c4 + 1) * 128],
                        idt[:, :]), waits=[(tv, sc.n(tv))], inc=(ttm, 1))
                    A("vector", lambda h, dc=dc, cdst=cdst: h.tensor_copy(
                        out=cdst[:, dc * 128:(dc + 1) * 128], in_=ps_t[:, :]),
                        waits=[(ttm, tt)], inc=(tv, 1))
            for rep in range(2):
                A("vector", lambda h, rep=rep: h.tensor_copy(
                    out=k1d[:, rep * 512:(rep + 1) * 512].rearrange(
                        "p (hh e) -> p hh e", e=64),
                    in_=ck_k.rearrange("p (hh t e) -> p hh t e", t=2, e=64)[:, :, 0, :]))
                A("vector", lambda h, rep=rep: h.tensor_copy(
                    out=k2d[:, rep * 512:(rep + 1) * 512].rearrange(
                        "p (hh e) -> p hh e", e=64),
                    in_=ck_k.rearrange("p (hh t e) -> p hh t e", t=2, e=64)[:, :, 1, :]))
                A("vector", lambda h, rep=rep: h.tensor_copy(
                    out=vd[:, rep * 1024:(rep + 1) * 1024], in_=ck_v[:, :]), inc=(tv, 1))
            for (awt, koff, qsel) in ((awt1, k1d, 0), (awt2, k2d, 1)):
                for o in range(8):
                    A("vector", lambda h, o=o, koff=koff, qsel=qsel: h.tensor_tensor(
                        out=prodt[:, :].rearrange("p (hh e) -> p hh e", e=64),
                        in0=ck_q.rearrange("p (hh t e) -> p hh t e", t=2, e=64)[:, :, qsel, :],
                        in1=koff[:, o * 64:o * 64 + 512].rearrange(
                            "p (hh e) -> p hh e", e=64), op=ALU.mult))
                    A("vector", lambda h, o=o, awt=awt: h.tensor_reduce(
                        out=awt[:, o * 8:(o + 1) * 8].unsqueeze(2),
                        in_=prodt[:, :].rearrange("p (hh e) -> p hh e", e=64),
                        axis=AX.X, op=ALU.add), inc=(tv, 1))
                A("vector", lambda h: h.memset(junkw[:, :], 0.0))
                A("vector", lambda h: h.memset(junkw[:, :], 0.0))
                A("vector", lambda h, awt=awt: h.tensor_reduce(
                    out=sgm[:, :].unsqueeze(2),
                    in_=awt[:, :].rearrange("p (o n) -> p n o", n=8),
                    axis=AX.X, op=ALU.max))
                A("vector", lambda h: h.memset(junkw[:, :], 0.0))
                A("vector", lambda h: h.memset(junkw[:, :], 0.0))
                vsub = A("vector", lambda h, awt=awt: h.tensor_tensor(
                    out=awt[:, :].rearrange("p (o n) -> p o n", n=8),
                    in0=awt[:, :].rearrange("p (o n) -> p o n", n=8),
                    in1=sgm[:, :].unsqueeze(1).broadcast_to((128, 8, 8)),
                    op=ALU.subtract), inc=(tv, 1))
                sex = A("scalar", lambda h, awt=awt: h.activation(
                    out=awt[:, :], in_=awt[:, :], func=AF.Exp, scale=float(SC_ATT)),
                    waits=[(tv, vsub)], inc=(tsc, 1))
                vsum = A("vector", lambda h, awt=awt: h.tensor_reduce(
                    out=sgm[:, :].unsqueeze(2),
                    in_=awt[:, :].rearrange("p (o n) -> p n o", n=8),
                    axis=AX.X, op=ALU.add), waits=[(tsc, sex)], inc=(tv, 1))
                sl_ = A("scalar", lambda h: h.activation(
                    out=sgm[:, :], in_=sgm[:, :], func=AF.Ln),
                    waits=[(tv, vsum)], inc=(tsc, 1))
                se_ = A("scalar", lambda h: h.activation(
                    out=sgm[:, :], in_=sgm[:, :], func=AF.Exp, scale=-1.0), inc=(tsc, 1))
                A("vector", lambda h, awt=awt: h.tensor_tensor(
                    out=awt[:, :].rearrange("p (o n) -> p o n", n=8),
                    in0=awt[:, :].rearrange("p (o n) -> p o n", n=8),
                    in1=sgm[:, :].unsqueeze(1).broadcast_to((128, 8, 8)),
                    op=ALU.mult), waits=[(tsc, se_)], inc=(tv, 1))
            for (awt, att) in ((awt1, at1), (awt2, at2)):
                for o in range(8):
                    A("vector", lambda h, o=o, awt=awt: h.tensor_tensor(
                        out=tmpw[:, :].rearrange("p (n e) -> p n e", e=128),
                        in0=vd[:, o * 128:o * 128 + 1024].rearrange(
                            "p (n e) -> p n e", e=128),
                        in1=awt[:, o * 8:(o + 1) * 8].unsqueeze(2).broadcast_to(
                            (128, 8, 128)), op=ALU.mult))
                    if o == 0:
                        A("vector", lambda h, att=att: h.tensor_copy(
                            out=att[:, :], in_=tmpw[:, :]), inc=(tv, 1))
                    else:
                        A("vector", lambda h, att=att: h.tensor_tensor(
                            out=att[:, :], in0=att[:, :], in1=tmpw[:, :], op=ALU.add),
                            inc=(tv, 1))
            A("vector", lambda h: h.scalar_tensor_tensor(
                out=at1[:, :], in0=at2[:, :], scalar=lamc[:, 0:1], in1=at1[:, :],
                op0=ALU.mult, op1=ALU.add), inc=(tv, 1))
            sq2 = A("scalar", lambda h: h.activation(
                out=tmpw[:, :], in_=at1[:, :], func=AF.Square),
                waits=[(tv, sc.n(tv))], inc=(tsc, 1))
            A("vector", lambda h: h.tensor_reduce(
                out=sgm[:, :].unsqueeze(2),
                in_=tmpw[:, :].rearrange("p (n e) -> p n e", e=128),
                axis=AX.X, op=ALU.add), waits=[(tsc, sq2)])
            A("vector", lambda h: h.memset(junkw[:, :], 0.0))
            A("vector", lambda h: h.memset(junkw[:, :], 0.0))
            vr2 = A("vector", lambda h: h.tensor_scalar(
                out=sgm[:, :], in0=sgm[:, :], scalar1=float(1.0 / 128.0),
                scalar2=float(1e-8), op0=ALU.mult, op1=ALU.add), inc=(tv, 1))
            sr1 = A("scalar", lambda h: h.activation(
                out=sgm[:, :], in_=sgm[:, :], func=AF.Ln), waits=[(tv, vr2)], inc=(tsc, 1))
            sr2 = A("scalar", lambda h: h.activation(
                out=sgm[:, :], in_=sgm[:, :], func=AF.Exp, scale=-0.5), inc=(tsc, 1))
            A("vector", lambda h: h.tensor_tensor(
                out=at1[:, :].rearrange("p (n e) -> p n e", e=128),
                in0=at1[:, :].rearrange("p (n e) -> p n e", e=128),
                in1=sgm[:, :].unsqueeze(2).broadcast_to((128, 8, 128)), op=ALU.mult),
                waits=[(tsc, sr2)])
            A("vector", lambda h: h.tensor_tensor(
                out=at1[:, :], in0=at1[:, :], in1=lnwB[:, :], op=ALU.mult))
            vaf = A("vector", lambda h: h.tensor_tensor(
                out=at1[:, :], in0=at1[:, :], in1=lnbB[:, :], op=ALU.add), inc=(tv, 1))
            for dc in range(8):
                tt = A("tensor", lambda h, dc=dc: h.transpose(
                    ps_t[:, :], at1[:, dc * 128:(dc + 1) * 128], idt[:, :]),
                    waits=[(tv, sc.n(tv))], inc=(ttm, 1))
                A("vector", lambda h, dc=dc, c4=c4: h.tensor_copy(
                    out=aFt[:, dc * 512 + c4 * 128:dc * 512 + (c4 + 1) * 128],
                    in_=ps_t[:, :]), waits=[(ttm, tt)], inc=(tv, 1))
        AFD = sc.n(tv)
        for jc in range(8):
            wl = A("sync", lambda h, jc=jc: h.dma_start(
                out=wbuf[:, :].rearrange("p (a c) -> p a c", a=8),
                in_=Wom[:, jc * 128:(jc + 1) * 128].rearrange("(a p) c -> p a c", p=128)),
                waits=[(ttm, sc.n(ttm))], inc=(ld, 16))

            def mmo(h, jc=jc):
                last = None
                for dc in range(8):
                    last = h.matmul(ps_m[:, 0:512], wbuf[:, dc * 128:(dc + 1) * 128],
                                    aFt[:, dc * 512:(dc + 1) * 512],
                                    start=(dc == 0), stop=(dc == 7))
                return last
            tmo = A("tensor", mmo, waits=[(ld, sc.n(ld)), (tv, AFD)], inc=(ttm, 1))
            A("vector", lambda h, jc=jc: h.tensor_copy(
                out=u5[:, jc * 512:(jc + 1) * 512], in_=ps_m[:, 0:512]),
                waits=[(ttm, tmo)] + ([(dout, sc.n(dout))] if jc == 0 else []),
                inc=(tv, 1))
        if L3_DEBUG:
            for jc in range(8):
                A("sync", lambda h, jc=jc, c0=c0, c1=c1: h.dma_start(
                    out=s1Dbg[jc * 128:(jc + 1) * 128, c0:c1],
                    in_=u5[:, jc * 512:(jc + 1) * 512]),
                    waits=[(tv, sc.n(tv))] if jc == 0 else (), inc=(dout, 16))
                A("sync", lambda h, jc=jc, c0=c0, c1=c1: h.dma_start(
                    out=aDbg[jc * 128:(jc + 1) * 128, c0:c1],
                    in_=aFt[:, jc * 512:(jc + 1) * 512]), inc=(dout, 16))
                A("sync", lambda h, jc=jc, c0=c0, c1=c1: h.dma_start(
                    out=qDbg[jc * 128:(jc + 1) * 128, c0:c1],
                    in_=u0[:, jc * 512:(jc + 1) * 512]), inc=(dout, 16))
        for hc in range(32):
            wl = A("sync", lambda h, hc=hc: h.dma_start(
                out=wbuf[:, :].rearrange("p (a c) -> p a c", a=8),
                in_=WqfD[:, hc * 128:(hc + 1) * 128].rearrange("(a p) c -> p a c", p=128)),
                waits=[(ttm, sc.n(ttm))], inc=(ld, 16))

            def mmf(h, hc=hc):
                last = None
                for dc in range(8):
                    last = h.matmul(ps_m[:, 0:512], wbuf[:, dc * 128:(dc + 1) * 128],
                                    xnt[:, dc * 512:(dc + 1) * 512],
                                    start=(dc == 0), stop=(dc == 7))
                return last
            tmf = A("tensor", mmf, waits=[(ld, sc.n(ld)), (tv, sc.n(tv)),
                                          (tsc, sc.n(tsc))], inc=(ttm, 1))
            A("scalar", lambda h, hc=hc: h.activation(
                out=h1f[:, hc * 512:(hc + 1) * 512], in_=ps_m[:, 0:512],
                func=AF.Gelu), waits=[(ttm, tmf)], inc=(tsc, 1))
        FCD = sc.n(tsc)
        def mmls(h):
            last = None
            for hc in range(32):
                last = h.matmul(ps_s1[0:1, 0:512], onesc[:, :],
                                h1f[:, hc * 512:(hc + 1) * 512],
                                start=(hc == 0), stop=(hc == 31))
            return last
        tls = A("tensor", mmls, waits=[(tsc, FCD)], inc=(ttm, 1))
        for hc in range(32):
            A("scalar", lambda h, hc=hc: h.activation(
                out=u0[:, (hc % 8) * 512:(hc % 8 + 1) * 512],
                in_=h1f[:, hc * 512:(hc + 1) * 512],
                func=AF.Square), waits=[(ttm, sc.n(ttm))] if hc % 8 == 0 else (),
                inc=(tsc, 1))
            if hc % 8 == 7:
                def mmls2(h, hq=hc // 8):
                    last = None
                    for k in range(8):
                        last = h.matmul(ps_s2[0:1, 0:512], onesc[:, :],
                                        u0[:, k * 512:(k + 1) * 512],
                                        start=(hq == 0 and k == 0),
                                        stop=(hq == 3 and k == 7))
                    return last
                A("tensor", mmls2, waits=[(tsc, sc.n(tsc))], inc=(ttm, 1))
        A("vector", lambda h: h.memset(junkw[:, :], 0.0), waits=[(ttm, sc.n(ttm))])
        A("vector", lambda h: h.memset(junkw[:, :], 0.0))
        A("vector", lambda h: h.tensor_scalar(
            out=musb[:, :], in0=ps_s1[0:1, 0:512], scalar1=float(1.0 / 4096.0),
            scalar2=None, op0=ALU.mult))
        A("vector", lambda h: h.tensor_scalar(
            out=varsb[:, :], in0=ps_s2[0:1, 0:512], scalar1=float(1.0 / 4096.0),
            scalar2=None, op0=ALU.mult))
        A("vector", lambda h: h.memset(junkw[:, :], 0.0))
        A("vector", lambda h: h.tensor_tensor(
            out=ytmp[0:1, :], in0=musb[:, :], in1=musb[:, :], op=ALU.mult))
        A("vector", lambda h: h.memset(junkw[:, :], 0.0))
        A("vector", lambda h: h.memset(junkw[:, :], 0.0))
        vva = A("vector", lambda h: h.tensor_tensor(
            out=varsb[:, :], in0=varsb[:, :], in1=ytmp[0:1, :], op=ALU.subtract),
            inc=(tv, 1))
        sva = A("scalar", lambda h: h.activation(
            out=rstd[:, :], in_=varsb[:, :], func=AF.Ln, bias=eps5t[0:1, 0:1]),
            waits=[(tv, vva)], inc=(tsc, 1))
        svb = A("scalar", lambda h: h.activation(
            out=rstd[:, :], in_=rstd[:, :], func=AF.Exp, scale=-0.5), inc=(tsc, 1))
        tb1 = A("tensor", lambda h: h.matmul(ps_m[:, 0:512], onesr[:, :], musb[:, :],
                                             start=True, stop=True),
                waits=[(tv, vva), (tsc, svb)], inc=(ttm, 1))
        tb2 = A("tensor", lambda h: h.matmul(ps_a[:, 0:512], onesr[:, :], rstd[:, :],
                                             start=True, stop=True), inc=(ttm, 1))
        A("vector", lambda h: h.tensor_copy(out=muB[:, :], in_=ps_m[:, 0:512]),
          waits=[(ttm, tb2)])
        vmu2 = A("vector", lambda h: h.tensor_copy(out=rstdB[:, :], in_=ps_a[:, 0:512]),
                 inc=(tv, 1))
        for hc in range(32):
            bw = A("sync", lambda h, hc=hc: h.dma_start(
                out=thq[:, 0:1], in_=l1w[hc * 128:(hc + 1) * 128, :]),
                waits=[(tv, sc.n(tv))], inc=(ld, 16))
            bb = A("sync", lambda h, hc=hc: h.dma_start(
                out=thq[:, 1:2], in_=l1b[hc * 128:(hc + 1) * 128, :]), inc=(ld, 16))
            A("vector", lambda h, hc=hc: h.tensor_tensor(
                out=h1f[:, hc * 512:(hc + 1) * 512], in0=h1f[:, hc * 512:(hc + 1) * 512],
                in1=muB[:, :], op=ALU.subtract),
                waits=[(tv, vmu2), (ld, bb)])
            A("vector", lambda h, hc=hc: h.tensor_tensor(
                out=h1f[:, hc * 512:(hc + 1) * 512], in0=h1f[:, hc * 512:(hc + 1) * 512],
                in1=rstdB[:, :], op=ALU.mult))
            A("vector", lambda h, hc=hc: h.tensor_tensor(
                out=h1f[:, hc * 512:(hc + 1) * 512], in0=h1f[:, hc * 512:(hc + 1) * 512],
                in1=thq[:, 0:1].broadcast_to((128, 512)), op=ALU.mult))
            A("vector", lambda h, hc=hc: h.tensor_tensor(
                out=h1f[:, hc * 512:(hc + 1) * 512], in0=h1f[:, hc * 512:(hc + 1) * 512],
                in1=thq[:, 1:2].broadcast_to((128, 512)), op=ALU.add), inc=(tv, 1))
        LND = sc.n(tv)
        for jc in range(8):
            def emit_proj(jc=jc):
                for hq in range(4):
                    wl = A("sync", lambda h, jc=jc, hq=hq: h.dma_start(
                        out=wbuf[:, :].rearrange("p (a c) -> p a c", a=8),
                        in_=WqpD[hq * 1024:(hq + 1) * 1024, jc * 128:(jc + 1) * 128].rearrange(
                            "(a p) c -> p a c", p=128)),
                        waits=[(ttm, sc.n(ttm))], inc=(ld, 16))

                    def mmp(h, jc=jc, hq=hq):
                        last = None
                        for k in range(8):
                            last = h.matmul(
                                ps_m[:, 0:512], wbuf[:, k * 128:(k + 1) * 128],
                                h1f[:, (hq * 8 + k) * 512:(hq * 8 + k + 1) * 512],
                                start=(hq == 0 and k == 0), stop=(hq == 3 and k == 7))
                        return last
                    A("tensor", mmp, waits=[(ld, sc.n(ld)), (tv, LND), (tsc, sc.n(tsc))],
                      inc=(ttm, 1))
                A("scalar", lambda h, jc=jc: h.activation(
                    out=u1[:, jc * 512:(jc + 1) * 512], in_=ps_m[:, 0:512],
                    func=AF.Silu), waits=[(ttm, sc.n(ttm))], inc=(tsc, 1))
            emit_proj()
        if L3_DEBUG:
            for jc in range(8):
                A("sync", lambda h, jc=jc, c0=c0, c1=c1: h.dma_start(
                    out=g1Dbg[jc * 128:(jc + 1) * 128, c0:c1],
                    in_=u1[:, jc * 512:(jc + 1) * 512]),
                    waits=[(tsc, sc.n(tsc))] if jc == 0 else (), inc=(dout, 16))
        for jc in range(8):
            A("vector", lambda h, jc=jc: h.tensor_tensor(
                out=u5[:, jc * 512:(jc + 1) * 512], in0=u5[:, jc * 512:(jc + 1) * 512],
                in1=u1[:, jc * 512:(jc + 1) * 512], op=ALU.mult),
                waits=[(tsc, sc.n(tsc))] if jc == 0 else ())
            A("vector", lambda h, jc=jc: h.tensor_tensor(
                out=u5[:, jc * 512:(jc + 1) * 512], in0=u5[:, jc * 512:(jc + 1) * 512],
                in1=u3[:, jc * 512:(jc + 1) * 512], op=ALU.add), inc=(tv, 1))
            A("sync", lambda h, jc=jc, c0=c0, c1=c1: h.dma_start(
                out=outF[jc * 128:(jc + 1) * 128, c0:c1],
                in_=u5[:, jc * 512:(jc + 1) * 512]),
                waits=[(tv, sc.n(tv))], inc=(dout, 16))

    A("sync", lambda h: h.nop(), waits=[(dout, sc.n(dout))])

    blk = nc.Block()
    ctx.append(blk)
    block = blk.__enter__()
    sc.emit(block)
    return ctx


def l3_host_inputs(pre_outs, scan_ys, moe_outs, ins, core):
    b, q = core // 4, core % 4
    t0 = q * TPC
    x = ins["L0_attn_x"][b, t0:t0 + TPC]
    ys = [scan_ys[b * 4 + cr].reshape(D, S)[:, t0:t0 + TPC] for cr in range(4)]
    imap = {
        "xTs": np.ascontiguousarray(x.T),
        "y0": np.ascontiguousarray(ys[0]), "y1": np.ascontiguousarray(ys[1]),
        "y2": np.ascontiguousarray(ys[2]), "y3": np.ascontiguousarray(ys[3]),
        "zFs": pre_outs[core]["zF"],
        "shF": moe_outs[core]["sharedF"],
        "Wout": ins["L0_attn_W_out"],
        "bout_c": np.ascontiguousarray(ins["L0_attn_b_out"].reshape(D, 1)),
        "Wqm": ins["L1_attn_Wq"], "Wkm": ins["L1_attn_Wk"],
        "Wvm": ins["L1_attn_Wv"], "Wom": ins["L1_attn_Wo"],
        "lq1": np.ascontiguousarray(ins["L1_attn_lambda_q1"].reshape(64, 1)),
        "lk1": np.ascontiguousarray(ins["L1_attn_lambda_k1"].reshape(64, 1)),
        "lq2": np.ascontiguousarray(ins["L1_attn_lambda_q2"].reshape(64, 1)),
        "lk2": np.ascontiguousarray(ins["L1_attn_lambda_k2"].reshape(64, 1)),
        "lnw8": np.ascontiguousarray(np.tile(ins["L1_attn_ln_w"], 8).reshape(1, D)),
        "lnb8": np.ascontiguousarray(np.tile(ins["L1_attn_ln_b"], 8).reshape(1, D)),
        "Jr": _jrot_const(),
        "identl": np.eye(128, dtype=np.float32),
        "WfcT": np.ascontiguousarray(ins["L1_ffn_W_fc"].T),
        "WprojT": np.ascontiguousarray(ins["L1_ffn_W_proj"].T),
        "l1w": np.ascontiguousarray(ins["L1_ffn_ln_w"].reshape(4096, 1)),
        "l1b": np.ascontiguousarray(ins["L1_ffn_ln_b"].reshape(4096, 1)),
    }
    for e in range(8):
        imap[f"R{e}"] = np.ascontiguousarray(
            moe_outs[e]["R_e"][b * S + t0:b * S + t0 + TPC, :])
    cos, sin = _rotary_tables(t0)
    imap["cosD"] = cos
    imap["sinD"] = sin
    return imap


def _jrot_const():
    M = np.zeros((128, 128), np.float32)
    for i in range(64):
        M[2 * i, 2 * i + 1] = -1.0
        M[2 * i + 1, 2 * i] = 1.0
    return np.ascontiguousarray(M.T)


def _rotary_tables(t0):
    d = 64
    inv_freq = (1.0 / 10000.0 ** (np.arange(0, d, 2, dtype=np.float64) / d))
    pos = np.arange(t0, t0 + TPC, dtype=np.float64)
    freqs = pos[:, None] * inv_freq[None, :]
    freqs = np.repeat(freqs, 2, axis=-1)
    cos = np.cos(freqs).astype(np.float32).T
    sin = np.sin(freqs).astype(np.float32).T
    return (np.ascontiguousarray(np.tile(cos, (2, 1))),
            np.ascontiguousarray(np.tile(sin, (2, 1))))


def kernel(**inputs):
    ins = dict(inputs)
    cores = list(range(8))

    nc1 = bass.Bass("TRN2", target_bir_lowering=False, debug=False,
                    detect_race_conditions=False)
    build_pre(nc1)
    r1 = run_bass_kernel_spmd(nc1, [pre_host_inputs(ins, c) for c in cores],
                              core_ids=cores)
    pre_outs = [dict(r) for r in r1.results]

    nc2 = bass.Bass("TRN2", target_bir_lowering=False, debug=False,
                    detect_race_conditions=False)
    build_scan2(nc2, S_steps=S)
    r2 = run_bass_kernel_spmd(nc2, [scan_host_inputs(pre_outs, ins, c, S)
                                    for c in cores], core_ids=cores)
    scan_ys = [r2.results[c]["y"] for c in cores]

    nc3 = bass.Bass("TRN2", target_bir_lowering=False, debug=False,
                    detect_race_conditions=False)
    build_moe(nc3)
    r3 = run_bass_kernel_spmd(nc3, [moe_host_inputs(pre_outs, ins, c)
                                    for c in cores], core_ids=cores)
    moe_outs = [dict(r) for r in r3.results]

    nc4 = bass.Bass("TRN2", target_bir_lowering=False, debug=False,
                    detect_race_conditions=False)
    build_l3(nc4)
    r4 = run_bass_kernel_spmd(nc4, [l3_host_inputs(pre_outs, scan_ys, moe_outs,
                                                   ins, c) for c in cores],
                              core_ids=cores)

    out = np.zeros((B, S, D), np.float32)
    for c in cores:
        b, q = c // 4, c % 4
        out[b, q * TPC:(q + 1) * TPC] = r4.results[c]["outF"].T
    return out
